# version 25
# speedup vs baseline: 1.2529x; 1.0195x over previous
import numpy as np
import ml_dtypes
from contextlib import ExitStack
import concourse.bass as bass
import concourse.mybir as mybir
from concourse.bass_utils import run_bass_kernel_spmd

F32 = mybir.dt.float32
BF16 = mybir.dt.bfloat16
AF = mybir.ActivationFunctionType
ALU = mybir.AluOpType
AX = mybir.AxisListType
BF = ml_dtypes.bfloat16

B, S, DM = 16, 2048, 1024
NCORES = 8
SPC = B // NCORES
NT = S // 128
NC = DM // 128
GT = 4
NG = NT // GT
EPS = 1e-5
BIG = np.float32(3.0e38)

F0_HEADS, F0_HD, F0_WIN, F0_INNER = 8, 64, 512, 512
F0_HID = 2730
F0_H2 = 5460
F0_KC2 = (F0_HID + 127) // 128
F0_LASTC = F0_HID - 128 * (F0_KC2 - 1)
A_HEADS, A_HD = 16, 64

FUSED = True
LAST_EM = None
DEBUG_DUMP = False
DETECT_RACES = True


def _fm(w):
    K, M = w.shape
    assert K % 128 == 0
    return np.ascontiguousarray(w.reshape(K // 128, 128, M).transpose(1, 0, 2))


def _perpart(v):
    n = v.shape[0]
    assert n % 128 == 0
    return np.ascontiguousarray(v.reshape(n // 128, 128).T)


def _bcast128(v):
    return np.ascontiguousarray(np.broadcast_to(v[None, :], (128, v.shape[0])))


def _xpos_tables():
    n, dh, win = S, F0_HD, F0_WIN
    half = np.arange(0, dh, 2, dtype=np.float64)
    inv_freq = 1.0 / 10000.0 ** (half / dh)
    t = np.arange(n, dtype=np.float64)
    freqs = t[:, None] * inv_freq[None, :]
    freqs = np.concatenate([freqs, freqs], 1)
    base = (half + 0.4 * dh) / (1.4 * dh)
    power = (t - n // 2) / float(win // 2)
    scale = base[None, :] ** power[:, None]
    scale = np.concatenate([scale, scale], 1)
    cos, sin = np.cos(freqs), np.sin(freqs)
    sgn = np.concatenate([-np.ones(32), np.ones(32)])[None, :]
    csq = (cos * scale * 8.0).astype(np.float32)
    ssq = (sin * scale * 8.0 * sgn).astype(np.float32)
    csk = (cos / scale).astype(np.float32)
    ssk = (sin / scale * sgn).astype(np.float32)
    def tm(a):
        return np.ascontiguousarray(a.reshape(NT, 128, 64).transpose(1, 0, 2))
    return tm(csq), tm(ssq), tm(csk), tm(ssk)


def prep_inputs(inp):
    d = {}
    d["idb"] = np.eye(128, dtype=BF)
    d["idf"] = np.eye(128, dtype=np.float32)
    triu = np.triu(np.ones((128, 128), bool), 1)
    d["mask_triu"] = np.where(triu, -BIG, 0).astype(BF)
    tril = np.tril(np.ones((128, 128), bool), -1)
    d["mask_tril"] = np.where(tril, -BIG, 0).astype(BF)
    csq, ssq, csk, ssk = _xpos_tables()
    d["xpos_csq"], d["xpos_ssq"], d["xpos_csk"], d["xpos_ssk"] = csq, ssq, csk, ssk
    d["iota6"] = _bcast128(np.tile(np.arange(6, dtype=np.float32), NT))
    d["cvec"] = _bcast128(np.concatenate([
        np.array([EPS, 1e-6, 0, 0], np.float32), np.ones(128, np.float32)]))
    d["big6"] = _bcast128(np.tile(np.arange(6, dtype=np.float32) - 100.0, NT))

    d["a0_wqkv"] = _fm(np.asarray(inp["L0_attn_qkv_w"]))
    d["a0_bqkv"] = _perpart(np.asarray(inp["L0_attn_qkv_b"])[:2048])
    d["a0_bv"] = _bcast128(np.asarray(inp["L0_attn_qkv_b"])[2048:])
    d["a0_wo"] = _fm(np.asarray(inp["L0_attn_out_w"]))
    d["a0_bo"] = _bcast128(np.asarray(inp["L0_attn_out_b"]))

    wqkv = np.asarray(inp["L0_ffn_wqkv"])
    wo = np.asarray(inp["L0_ffn_wo"])
    w1 = np.asarray(inp["L0_ffn_w1"])
    w2 = np.asarray(inp["L0_ffn_w2"])
    for l in range(2):
        d[f"f0_wqkv{l}"] = _fm(wqkv[l])
        d[f"f0_wo{l}"] = _fm(wo[l])
        d[f"f0_w1{l}"] = _fm(w1[l])
        w2p = np.zeros((F0_KC2 * 128, 1024), np.float32)
        w2p[:F0_HID] = w2[l]
        d[f"f0_w2{l}"] = _fm(w2p)
        b1p = np.zeros((F0_KC2 * 128 * 2,), np.float32)
        b1 = np.asarray(inp["L0_ffn_b1"])[l]
        b1p[:F0_HID] = b1[:F0_HID]
        b1p[F0_KC2 * 128: F0_KC2 * 128 + F0_HID] = b1[F0_HID:]
        d[f"f0_b1{l}"] = _perpart(b1p)
        d[f"f0_b2{l}"] = _bcast128(np.asarray(inp["L0_ffn_b2"])[l])
        d[f"f0_lng{l}"] = _bcast128(np.asarray(inp["L0_ffn_ln_g"])[l])
        d[f"f0_lnb{l}"] = _bcast128(np.asarray(inp["L0_ffn_ln_b"])[l])
        d[f"f0_lngp{l}"] = _perpart(np.asarray(inp["L0_ffn_ln_g"])[l])
        d[f"f0_lnbp{l}"] = _perpart(np.asarray(inp["L0_ffn_ln_b"])[l])
        qs = np.asarray(inp["L0_ffn_q_scale"])[l]
        ks = np.asarray(inp["L0_ffn_k_scale"])[l]
        d[f"f0_qs{l}"] = _bcast128(qs)
        d[f"f0_ks{l}"] = _bcast128(ks)

    Wq = np.asarray(inp["L1_attn_Wq"])
    Wk = np.asarray(inp["L1_attn_Wk"])
    Wvd = np.asarray(inp["L1_attn_Wvd"])
    Wvu = np.asarray(inp["L1_attn_Wvu"])
    d["a1_wq"] = _fm(np.concatenate([Wq[h] for h in range(16)], 1))
    d["a1_wk"] = _fm(np.concatenate([Wk[h] for h in range(16)], 1))
    d["a1_wvd"] = _fm(np.concatenate([Wvd[h] for h in range(16)], 1))
    vu = np.zeros((256, 1024), np.float32)
    for h in range(16):
        vu[16 * h:16 * h + 16, 64 * h:64 * h + 64] = Wvu[h]
    d["a1_vu"] = _fm(vu)
    d["a1_bq"] = _perpart(np.asarray(inp["L1_attn_bq"]).reshape(-1))
    d["a1_bk"] = _perpart(np.asarray(inp["L1_attn_bk"]).reshape(-1))
    d["a1_bvd"] = _perpart(np.asarray(inp["L1_attn_bvd"]).reshape(-1))
    d["a1_bvu"] = _bcast128(np.asarray(inp["L1_attn_bvu"]).reshape(-1))

    d["f1_basewt"] = _fm(np.ascontiguousarray(np.asarray(inp["L1_ffn_base_W"]).T))
    d["f1_baseb"] = _bcast128(np.asarray(inp["L1_ffn_base_b"]))
    rA = np.asarray(inp["L1_ffn_routed_A"])
    s96 = np.concatenate([rA[e].T for e in range(6)], 1)
    d["f1_s96"] = _fm(s96)
    d["f1_sa16"] = _fm(np.ascontiguousarray(np.asarray(inp["L1_ffn_shared_A"]).T))
    sbt = np.zeros((128, 1024), np.float32)
    sbt[0:16] = np.asarray(inp["L1_ffn_shared_B"]).T
    d["f1_sbt"] = sbt
    bt = np.zeros((128, 1024), np.float32)
    rB = np.asarray(inp["L1_ffn_routed_B"])
    for e in range(6):
        bt[16 * e:16 * e + 16] = rB[e].T
    d["f1_btb"] = bt
    d["f1_rt"] = _fm(np.ascontiguousarray(np.asarray(inp["L1_ffn_router_W"]).T))
    d["f1_rb"] = _bcast128(np.asarray(inp["L1_ffn_router_b"]))
    sel = np.zeros((32, 96), np.float32)
    for e in range(6):
        sel[e, 16 * e:16 * e + 16] = 1.0
    d["f1_sel"] = sel

    return d


ENG = ("tensor", "vector", "scalar", "gpsimd", "sync")
CHANNELS = ("const", "xio", "dbg", "w0", "w1", "wA", "wB", "wC", "wD", "tab",
            "cv", "cva")
SEMNAMES = ENG + CHANNELS


class Em:
    def __init__(self):
        self.q = {e: [] for e in ENG}
        self.cnt = {n: 0 for n in SEMNAMES}
        self.lastw = {c: {} for c in ENG}
        self.lab = ""
        self.nm = ""
        self.labels = {n: [] for n in SEMNAMES}
        self.names = {n: [] for n in SEMNAMES}

    def op(self, eng, fn, inc=1, w=None, ch=None, sw=None, nm=None):
        key = ch if ch is not None else eng
        self.labels[key].append(self.lab)
        self.names[key].append(nm or self.nm)
        assert key in self.cnt, key
        eff = []
        if w:
            for pe, v in w.items():
                if pe == eng or not v:
                    continue
                if pe in CHANNELS:
                    v = self.cnt[pe]
                if v > self.lastw[eng].get(pe, 0):
                    eff.append((pe, v))
                    self.lastw[eng][pe] = v
        if sw:
            if sw > self.lastw[eng].get(eng, 0):
                eff.append((eng, sw))
                self.lastw[eng][eng] = sw
        self.q[eng].append((eff, fn, inc, key))
        if inc:
            self.cnt[key] += inc
        return self.cnt[key]

    def tok(self):
        return dict(self.cnt)

    def emit(self, blk, sems):
        final = dict(self.cnt)
        for e in ENG:
            ops = self.q[e]
            def make(ops, e=e):
                def body(eng):
                    for eff, fn, inc, key in ops:
                        for pe, v in eff:
                            eng.wait_ge(sems[pe], v)
                        inst = fn(eng)
                        if inc:
                            inst.then_inc(sems[key], inc)
                    if e == "sync":
                        for pe in SEMNAMES:
                            if final[pe]:
                                eng.wait_ge(sems[pe], final[pe])
                return body
            getattr(blk, e)(make(ops))


def wmax(*ds):
    out = {}
    for d in ds:
        if not d:
            continue
        for k, v in d.items():
            if v and v > out.get(k, 0):
                out[k] = v
    return out


class Arena:
    LIMIT = 212832

    def __init__(self, nc, em):
        self.nc = nc
        self.em = em
        self.base = nc.alloc_sbuf_tensor("arena", [128, self.LIMIT // 4], F32)
        self.regions = {}
        self.free_list = [(0, self.LIMIT)]
        self.peak = 0
        self.ftok = {}
        self.pending = {}

    def alloc(self, name, shape, dtype, align=64):
        esz = 2 if dtype == BF16 else 4
        nbytes = int(np.prod(shape[1:])) * esz
        for i, (off, size) in enumerate(self.free_list):
            a = (off + align - 1) // align * align
            if a + nbytes <= off + size:
                self.free_list.pop(i)
                if a > off:
                    self.free_list.insert(i, (off, a - off))
                if off + size > a + nbytes:
                    self.free_list.append((a + nbytes, off + size - (a + nbytes)))
                self.free_list.sort()
                v = self.base[:, a // 4: (a + nbytes) // 4]
                if dtype == BF16:
                    v = v.bitcast(BF16)
                if len(shape) == 3:
                    v = v.rearrange("p (a b) -> p a b", b=shape[2])
                elif len(shape) == 4:
                    v = v.rearrange("p (a b c) -> p a b c", b=shape[2], c=shape[3])
                self.regions[id(v)] = (a, nbytes)
                self.pending[id(v)] = dict(self.ftok)
                self.peak = max(self.peak, a + nbytes)
                return v
        raise MemoryError(f"SBUF arena full allocating {name} ({nbytes}B); free={self.free_list}")

    def wfirst(self, t):
        return self.pending.get(id(t), {})

    def free(self, t):
        self.ftok = wmax(self.ftok, self.em.tok())
        off, nbytes = self.regions.pop(id(t))
        self.free_list.append((off, nbytes))
        self.free_list.sort()
        merged = []
        for seg in self.free_list:
            if merged and merged[-1][0] + merged[-1][1] == seg[0]:
                merged[-1] = (merged[-1][0], merged[-1][1] + seg[1])
            else:
                merged.append(list(seg))
        self.free_list = [tuple(x) for x in merged]



def build_nc(blocks):
    nc = bass.Bass("TRN2", target_bir_lowering=False, debug=False,
                   detect_race_conditions=DETECT_RACES)

    ap = {}
    def din(name, shape, dtype=F32):
        ap[name] = nc.dram_tensor(name, list(shape), dtype, kind="ExternalInput").ap()
        return ap[name]

    din("x_in", [SPC, S, DM])
    x_out = nc.dram_tensor("x_out", [SPC, S, DM], F32, kind="ExternalOutput").ap()

    din("idb", [128, 128], BF16)
    din("idf", [128, 128])
    din("mask_triu", [128, 128], BF16)
    din("mask_tril", [128, 128], BF16)
    din("cvec", [128, 132])
    need = set(blocks)
    if "A0" in need:
        din("a0_wqkv", [128, 8, 3072]); din("a0_bqkv", [128, 16]); din("a0_bv", [128, 1024])
        din("a0_wo", [128, 8, 1024]); din("a0_bo", [128, 1024])
    if "F0" in need:
        for l in range(2):
            din(f"f0_wqkv{l}", [128, 8, 1536]); din(f"f0_wo{l}", [128, 4, 1024])
            din(f"f0_w1{l}", [128, 8, 5460]); din(f"f0_w2{l}", [128, F0_KC2, 1024])
            din(f"f0_b1{l}", [128, 2 * F0_KC2]); din(f"f0_b2{l}", [128, 1024])
            din(f"f0_lng{l}", [128, 1024]); din(f"f0_lnb{l}", [128, 1024])
            din(f"f0_lngp{l}", [128, 8]); din(f"f0_lnbp{l}", [128, 8])
            din(f"f0_qs{l}", [128, 64]); din(f"f0_ks{l}", [128, 64])
        din("xpos_csq", [128, NT, 64]); din("xpos_ssq", [128, NT, 64])
        din("xpos_csk", [128, NT, 64]); din("xpos_ssk", [128, NT, 64])
    wcvt = {}
    if "F0" in need:
        for l in range(2):
            wcvt[f"w1b{l}"] = nc.dram_tensor(
                f"w1b{l}", [128, 8 * 5460], BF16, kind="Internal").ap()
            wcvt[f"w2b{l}"] = nc.dram_tensor(
                f"w2b{l}", [128, F0_KC2 * 1024], BF16, kind="Internal").ap()
    if "A1" in need:
        din("a1_wq", [128, 8, 1024]); din("a1_wk", [128, 8, 1024])
        din("a1_wvd", [128, 8, 256]); din("a1_vu", [128, 2, 1024])
        din("a1_bq", [128, 8]); din("a1_bk", [128, 8])
        din("a1_bvd", [128, 2]); din("a1_bvu", [128, 1024])
    if "F1" in need:
        din("f1_basewt", [128, 8, 1024]); din("f1_baseb", [128, 1024])
        din("f1_s96", [128, 8, 96]); din("f1_sa16", [128, 8, 16])
        din("f1_sbt", [128, 1024]); din("f1_btb", [128, 1024])
        din("f1_rt", [128, 8, 6]); din("f1_rb", [128, 6])
        din("f1_sel", [32, 96]); din("iota6", [128, 96]); din("big6", [128, 96])

    dbg = {}
    if DEBUG_DUMP:
        for nm, shape in [("dbg_x", [128, 1024]), ("dbg_raw", [128, 64]),
                          ("dbg_stats", [128, 64]), ("dbg_h", [128, 8, 512]),
                          ("dbg_k", [128, 8, 2048]), ("dbg_v", [128, 16, 1024]),
                          ("dbg_q", [128, 8, 512]), ("dbg_att", [128, 2048]),
                          ("dbg_oa", [128, 8, 512]), ("dbg_y", [128, 8, 512])]:
            dbg[nm] = nc.dram_tensor(nm, shape, F32, kind="ExternalOutput").ap()

    E = Em()
    with ExitStack() as ctx:
        ent = ctx.enter_context
        sems = {e: ent(nc.semaphore("sem_" + e)) for e in SEMNAMES}

        A = Arena(nc, E)
        x_sb = A.alloc("x", [128, NT, DM], F32)
        idb = A.alloc("idb", [128, 128], BF16)
        m_triu = A.alloc("mtriu", [128, 128], BF16)
        m_tril = A.alloc("mtril", [128, 128], BF16)
        cst = A.alloc("cst", [128, 132], F32)
        st = A.alloc("stats", [128, 256], F32)
        hbf = A.alloc("hbf", [128, 2, DM], BF16)
        sq_scr = A.alloc("sqscr", [128, DM], F32)

        postln_idf = [None]
        PS = ent(nc.psum_tensor("PS", [128, 4096], F32))
        PSB = PS[:].bitcast(BF16)

        def bank(b, n=1):
            return PS[:, 512 * b: 512 * b + 512 * n]

        def bankb(b, n=1):
            return PSB[:, 1024 * b: 1024 * b + 1024 * n]

        for dst, src in [(idb, "idb"), (m_triu, "mask_triu"), (m_tril, "mask_tril"), (cst, "cvec")]:
            E.op("sync", lambda e, dst=dst, src=src: e.dma_start(out=dst[:], in_=ap[src]), 16, ch="const")
        tload = E.tok()

        cvt_state = {"buf": None, "work": [], "i": 0}
        if wcvt:
            cvt = A.alloc("cvt", [128, 2, 512], BF16)
            cvt_state["buf"] = cvt
            CW = 512
            for l in range(2):
                for srcname, dstname, ncols in (
                        (f"f0_w1{l}", f"w1b{l}", 8 * 5460),
                        (f"f0_w2{l}", f"w2b{l}", F0_KC2 * 1024)):
                    srcf = ap[srcname].rearrange("p a b -> p (a b)")
                    dst = wcvt[dstname]
                    for c in range((ncols + CW - 1) // CW):
                        c0 = c * CW
                        cvt_state["work"].append(
                            (srcf, dst, c0, min(CW, ncols - c0)))

        def cvt_step(npairs=2):
            stch = ("cv", "cva")
            steng = ("sync", "sync")
            cvt = cvt_state["buf"]
            for _ in range(npairs):
                i = cvt_state["i"]
                if i >= len(cvt_state["work"]):
                    return
                cvt_state["i"] = i + 1
                srcf, dst, c0, w_ = cvt_state["work"][i]
                sl = i % 2
                tl = E.op("gpsimd", lambda e, sl=sl, c0=c0, w_=w_,
                          srcf=srcf: e.dma_start(
                              out=cvt[:, sl, 0:w_],
                              in_=srcf[:, c0:c0 + w_]), 16,
                          wmax(A.wfirst(cvt), {stch[sl]: 1}), ch="w1")
                E.op(steng[sl], lambda e, sl=sl, c0=c0, w_=w_,
                     dst=dst: e.dma_start(
                         out=dst[:, c0:c0 + w_],
                         in_=cvt[:, sl, 0:w_]), 16,
                     {"w1": tl}, ch=stch[sl])

        def ln_stats(src3, tiles, cols):
            c_sum, c_sq, c_rstd, c_nmrs = cols
            tk = E.tok()
            for i, t in enumerate(tiles):
                E.op("vector", lambda e, t=t, i=i: e.reduce_sum(
                    st[:, c_sum + i:c_sum + i + 1], src3[:, t, :], axis=AX.X), 1, tk,
                    sw=E.cnt["vector"])
                E.op("scalar", lambda e, t=t, i=i: e.activation(
                    sq_scr[:], src3[:, t, :], AF.Square,
                    accum_out=st[:, c_sq + i:c_sq + i + 1]), 1, tk,
                    sw=E.cnt["scalar"])
            n = len(tiles)
            w = E.tok()
            if DEBUG_DUMP:
                E.op("vector", lambda e: e.tensor_copy(st[:, 192:224], st[:, 0:32]), 1, w)
                dump("dbg_raw", st[:, 192:256], E.tok())
            t1 = E.op("vector", lambda e: e.tensor_scalar_mul(
                st[:, c_sum:c_sum + n], st[:, c_sum:c_sum + n], 1.0 / DM), 1, w,
                sw=E.cnt["vector"])
            t2 = E.op("vector", lambda e: e.tensor_scalar_mul(
                st[:, c_sq:c_sq + n], st[:, c_sq:c_sq + n], 1.0 / DM), 1)
            t3 = E.op("vector", lambda e: e.scalar_tensor_tensor(
                st[:, 240:240 + n], st[:, c_sum:c_sum + n], 1.0,
                st[:, c_sum:c_sum + n], ALU.mult, ALU.mult), 1, sw=t1)
            t4 = E.op("vector", lambda e: e.tensor_sub(
                st[:, c_sq:c_sq + n], st[:, c_sq:c_sq + n], st[:, 240:240 + n]), 1, sw=t3)
            wv = E.tok()
            tl = E.op("scalar", lambda e: e.activation(
                st[:, c_rstd:c_rstd + n], st[:, c_sq:c_sq + n], AF.Ln, bias=cst[:, 0:1]), 1, wv)
            ws = E.op("scalar", lambda e: e.activation(
                st[:, c_rstd:c_rstd + n], st[:, c_rstd:c_rstd + n], AF.Exp, scale=-0.5), 1, sw=tl)
            E.op("vector", lambda e: e.scalar_tensor_tensor(
                st[:, c_nmrs:c_nmrs + n], st[:, c_sum:c_sum + n], -1.0,
                st[:, c_rstd:c_rstd + n], ALU.mult, ALU.mult), 1, {"scalar": ws}, sw=t4)
            return E.tok()

        STC = (0, 16, 32, 48)

        def h_tile(src3, t, i, dst_fm, dst_col, wstats, dst_nc=NC, extra_w=None):
            c_rstd, c_nmrs = STC[2], STC[3]
            w = dict(wstats)
            if extra_w:
                w.update(extra_w)
            hb = hbf[:, t % 2, :]
            ta = E.op("scalar", lambda e, hb=hb: e.activation(
                hb, src3[:, t, :], AF.Identity,
                bias=st[:, c_nmrs + i:c_nmrs + i + 1],
                scale=st[:, c_rstd + i:c_rstd + i + 1]), 1, w)
            pb = 4 + (t % 2)
            wtr = {"scalar": ta, "vector": E.cnt["vector"]}
            for c in range(dst_nc):
                E.op("tensor", lambda e, c=c, pb=pb, hb=hb: e.transpose(
                    bankb(pb)[:, c * 128:(c + 1) * 128],
                    hb[:, c * 128:(c + 1) * 128], idb[:]), 1, wtr)
            tt = E.tok()
            return E.op("vector", lambda e, pb=pb: e.tensor_copy(
                dst_fm[:, 0:dst_nc, dst_col:dst_col + 128],
                bankb(pb)[:, 0:dst_nc * 128].rearrange("p (c q) -> p c q", q=128)), 1, tt)

        def wload(dst, src_ap, w=None, ch="w0", eng="gpsimd"):
            return E.op(eng, lambda e: e.dma_start(out=dst, in_=src_ap),
                        16, w, ch=ch)

        def mm_fm(ps_out, w_sb, mc, rhs_fm, c0, n, kcs, extra_lw=None, wr=None):
            def fn(e):
                last = None
                for j, kc in enumerate(kcs):
                    last = e.matmul(ps_out, w_sb[:, kc, mc * 128:(mc + 1) * 128],
                                    rhs_fm[:, kc, c0:c0 + n],
                                    start=(j == 0), stop=(j == len(kcs) - 1))
                return last
            return E.op("tensor", fn, 1, wr)


        def postln_group(y_fm, tile0, n_tiles=GT):
            c_sum, c_sq, c_rstd, c_nmrs = 64, 68, 72, 76
            ytile = PS[:, 3072:4096]
            for i in range(n_tiles):
                t = tile0 + i
                wtr = {"vector": E.cnt["vector"], "scalar": E.cnt["scalar"]}
                for c in range(NC):
                    E.op("tensor", lambda e, c=c, i=i: e.transpose(
                        PS[:, 3072 + c * 128: 3072 + (c + 1) * 128],
                        y_fm[:, c, i * 128:(i + 1) * 128], postln_idf[0][:]),
                        1, wtr)
                tt = E.tok()
                E.op("vector", lambda e, i=i: e.reduce_sum(
                    st[:, c_sum + i:c_sum + i + 1], ytile, axis=AX.X), 1, tt)
                E.op("scalar", lambda e, i=i: e.activation(
                    sq_scr[:], ytile, AF.Square,
                    accum_out=st[:, c_sq + i:c_sq + i + 1]), 1,
                    {**tt, "vector": E.cnt["vector"]}, sw=E.cnt["scalar"])
                wv = E.tok()
                t1 = E.op("vector", lambda e, i=i: e.tensor_scalar_mul(
                    st[:, c_sum + i:c_sum + i + 1], st[:, c_sum + i:c_sum + i + 1], 1.0 / DM), 1, wv,
                    sw=E.cnt["vector"])
                t2 = E.op("vector", lambda e, i=i: e.tensor_scalar_mul(
                    st[:, c_sq + i:c_sq + i + 1], st[:, c_sq + i:c_sq + i + 1], 1.0 / DM), 1)
                t3 = E.op("vector", lambda e, i=i: e.scalar_tensor_tensor(
                    st[:, 240:241], st[:, c_sum + i:c_sum + i + 1], 1.0,
                    st[:, c_sum + i:c_sum + i + 1], ALU.mult, ALU.mult), 1, sw=t1)
                t4 = E.op("vector", lambda e, i=i: e.tensor_sub(
                    st[:, c_sq + i:c_sq + i + 1], st[:, c_sq + i:c_sq + i + 1],
                    st[:, 240:241]), 1, sw=t3)
                wv2 = E.tok()
                tl = E.op("scalar", lambda e, i=i: e.activation(
                    st[:, c_rstd + i:c_rstd + i + 1], st[:, c_sq + i:c_sq + i + 1],
                    AF.Ln, bias=cst[:, 0:1]), 1, wv2)
                ws = E.op("scalar", lambda e, i=i: e.activation(
                    st[:, c_rstd + i:c_rstd + i + 1], st[:, c_rstd + i:c_rstd + i + 1],
                    AF.Exp, scale=-0.5), 1, sw=tl)
                E.op("vector", lambda e, i=i: e.scalar_tensor_tensor(
                    st[:, c_nmrs + i:c_nmrs + i + 1], st[:, c_sum + i:c_sum + i + 1], -1.0,
                    st[:, c_rstd + i:c_rstd + i + 1], ALU.mult, ALU.mult), 1,
                    {"scalar": ws}, sw=t4)
                wr = E.tok()
                ta = E.op("scalar", lambda e, ytile=ytile, i=i: e.activation(
                    sq_scr[:], ytile, AF.Identity,
                    bias=st[:, c_nmrs + i:c_nmrs + i + 1],
                    scale=st[:, c_rstd + i:c_rstd + i + 1]), 1, wr)
                E.op("vector", lambda e, t=t: e.tensor_add(
                    x_sb[:, t, :], x_sb[:, t, :], sq_scr[:]), 1, {"scalar": ta})

        def dump(nm, src_ap, w):
            if DEBUG_DUMP and nm in dbg:
                d = dbg.pop(nm)
                E.op("gpsimd", lambda e, d=d, s=src_ap: e.dma_start(out=d, in_=s), 16, w, ch="dbg")

        def emit_A0(seq):
            E.lab = f"{seq}:A0.h"
            tstat = ln_stats(x_sb[:], range(NT), STC)
            dump("dbg_stats", st[:, 0:64], tstat)
            hfull = A.alloc("a0h", [128, NC, S], BF16)
            for t in range(NT):
                h_tile(x_sb[:], t, t, hfull, t * 128, tstat,
                       extra_w={"tensor": E.cnt["tensor"]})
                cvt_step()
            dump("dbg_h", hfull[:], E.tok())
            E.lab = f"{seq}:A0.kv"
            wkv = A.alloc("a0wkv", [128, 8, 1024], BF16)
            kfm = A.alloc("a0k", [128, NC, S], BF16)
            vtok = A.alloc("a0v", [128, NT, DM], BF16)
            bqkv = A.alloc("a0bqkv", [128, 16], F32)
            bv = A.alloc("a0bv", [128, 1024], F32)
            twkv = wload(wkv[:], ap["a0_wqkv"][:, :, 1024:2048], w=A.wfirst(wkv))
            tbq = E.op("sync", lambda e: e.dma_start(out=bqkv[:], in_=ap["a0_bqkv"]), 16,
                       w=A.wfirst(bqkv), ch="const")
            tbv = E.op("sync", lambda e: e.dma_start(out=bv[:], in_=ap["a0_bv"]), 16,
                       w=A.wfirst(bv), ch="const")
            E.op("vector", lambda e: e.tensor_scalar_mul(
                bqkv[:, 0:8], bqkv[:, 0:8], 0.125), 1, {"const": tbq})
            for mc in range(8):
                for n in range(4):
                    pb = (mc * 4 + n) % 2
                    wk = {"vector": E.cnt["vector"], "scalar": E.cnt["scalar"],
                          "w0": twkv, "const": tbv}
                    tm = mm_fm(bank(pb), wkv, mc, hfull, n * 512, 512, range(8), wr=wk)
                    E.op("scalar", lambda e, pb=pb, mc=mc, n=n: e.activation(
                        kfm[:, mc, n * 512:(n + 1) * 512], bank(pb), AF.Identity,
                        bias=bqkv[:, 8 + mc:9 + mc]), 1,
                        wmax(A.wfirst(kfm), {"tensor": tm}))
                cvt_step()
            twv = wload(wkv[:], ap["a0_wqkv"][:, :, 2048:3072],
                        w={"tensor": E.cnt["tensor"]})
            for i in range(NT):
                for half in range(2):
                    pb = 2 + (i * 2 + half) % 2
                    wv_ = {"scalar": E.cnt["scalar"], "vector": E.cnt["vector"],
                           "w0": twv}
                    def vmm(e, i=i, half=half, pb=pb):
                        last = None
                        for kc in range(8):
                            last = e.matmul(
                                bank(pb), hfull[:, kc, i * 128:(i + 1) * 128],
                                wkv[:, kc, half * 512: half * 512 + 512],
                                start=(kc == 0), stop=(kc == 7))
                        return last
                    tv = E.op("tensor", vmm, 1, wv_)
                    E.op("vector", lambda e, i=i, half=half, pb=pb: e.scalar_tensor_tensor(
                        vtok[:, i, half * 512:half * 512 + 512],
                        bank(pb), 1.0, bv[:, half * 512: half * 512 + 512],
                        ALU.mult, ALU.add), 1, wmax(A.wfirst(vtok), {"tensor": tv}))
                cvt_step()
            dump("dbg_k", kfm[:], E.tok())
            dump("dbg_v", vtok[:], E.tok())
            tfree = E.tok()
            A.free(wkv)
            A.free(bv)
            A.free(hfull)
            wo_sb = A.alloc("a0wo", [128, 8, 1024], BF16)
            bo_b = A.alloc("a0bo", [128, 1024], F32)
            att2 = A.alloc("a0att", [128, 3, S], BF16)
            attTg = A.alloc("a0attT", [128, NT, 256], BF16)
            dg3 = A.alloc("a0dg", [128, 3, 128], BF16)
            oa = A.alloc("a0oa", [128, NC, 256], BF16)
            qhalf = A.alloc("a0qf", [128, NC, 512], BF16)
            two = wload(wo_sb[:], ap["a0_wo"], w=wmax(A.wfirst(wo_sb), tfree), ch="w0")
            tbo = E.op("sync", lambda e: e.dma_start(out=bo_b[:], in_=ap["a0_bo"]), 16,
                       w=A.wfirst(bo_b), ch="const")
            tm_sl = [0, 0]
            hh = A.alloc("a0hh", [128, NC, 512], BF16)
            wq_sl = [A.alloc("a0wqA", [128, 8, 128], BF16),
                     A.alloc("a0wqB", [128, 8, 128], BF16)]
            for quarter in range(4):
                E.lab = f"{seq}:A0.q"
                t0_h = quarter * 4
                for i in range(4):
                    h_tile(x_sb[:], t0_h + i, t0_h + i, hh, i * 128, tstat,
                           extra_w={"tensor": E.cnt["tensor"]})
                for mc in range(8):
                    sl = mc % 2
                    chn = "wA" if sl == 0 else "wB"
                    tw = wload(wq_sl[sl][:],
                               ap["a0_wqkv"][:, :, mc * 128:(mc + 1) * 128],
                               w=wmax(A.wfirst(wq_sl[sl]), {"tensor": tm_sl[sl]}),
                               ch=chn)
                    pb = mc % 2
                    wq_ = {chn: tw, "vector": E.cnt["vector"],
                           "scalar": E.cnt["scalar"]}
                    tm = mm_fm(bank(pb), wq_sl[sl][:], 0, hh, 0, 512,
                               range(8), wr=wq_)
                    tm_sl[sl] = tm
                    E.op("scalar", lambda e, pb=pb, mc=mc: e.activation(
                        qhalf[:, mc, :], bank(pb), AF.Identity,
                        bias=bqkv[:, mc:mc + 1], scale=0.125), 1,
                        wmax(A.wfirst(qhalf), {"tensor": tm}))
                    cvt_step()
                attn_sync_banks()
                for g2 in range(2):
                    E.lab = f"{seq}:A0.att"
                    g = quarter * 2 + g2
                    qgv = qhalf[:, :, g2 * 256:(g2 + 1) * 256]

                    def a0_evac(hc, aoff, wav):
                        E.op("vector", lambda e, hc=hc, aoff=aoff: e.tensor_copy(
                            oa[:, hc, :], bank(7)[:, aoff:aoff + 256]), 1,
                            wmax(A.wfirst(oa), wav), nm="oe")
                        cvt_step(2)
                    attn_core(A_HEADS, 2, 2 * g, qgv, kfm, vtok, True, att2, attTg,
                              dg3, a0_evac, depth=3)
                    E.lab = f"{seq}:A0.out"
                    for qi in range(2):
                        t = 2 * g + qi
                        wup = {"vector": E.cnt["vector"], "w0": two,
                               "scalar": E.cnt["scalar"]}
                        def upmm(e, qi=qi):
                            last = None
                            for half in range(2):
                                dst = PS[:, 1024 + half * 512: 1536 + half * 512]
                                for ic in range(8):
                                    last = e.matmul(
                                        dst, oa[:, ic, qi * 128:(qi + 1) * 128],
                                        wo_sb[:, ic, half * 512:half * 512 + 512],
                                        start=(ic == 0), stop=(ic == 7))
                            return last
                        tup = E.op("tensor", upmm, 1, wup, nm="up")
                        tbb = E.op("vector", lambda e: e.tensor_tensor(
                            PS[:, 1024:2048], PS[:, 1024:2048], bo_b[:], ALU.add),
                            1, {"tensor": tup, "const": tbo}, sw=E.cnt["vector"])
                        E.nm = "pl"
                        postln_psum_pair(PS[:, 1024:2048], t, {"vector": tbb})
                        E.nm = ""
                    attn_sync_banks()
            for t_ in (kfm, vtok, bqkv, qhalf, att2, attTg, dg3, oa, wo_sb, bo_b,
                       hh, wq_sl[0], wq_sl[1]):
                A.free(t_)

        attn_state = {"scores_par": 0, "tr_par": 0, "av_par": 0,
                      "lastexp": [0, 0], "lastmax": [0, 0],
                      "trw": [None, None, None], "avw": [None, None], "gev": {},
                      "att2r": [0, 0, 0, 0]}

        def attn_sync_banks():
            attn_state["lastexp"] = [E.cnt["scalar"], E.cnt["scalar"]]
            attn_state["lastmax"] = [E.cnt["vector"], E.cnt["vector"]]
            full = {"vector": E.cnt["vector"], "scalar": E.cnt["scalar"],
                    "gpsimd": E.cnt["gpsimd"]}
            attn_state["trw"] = [dict(full), dict(full), dict(full)]
            attn_state["avw"] = [dict(full), dict(full)]

        def attn_core(nheads, gt, tiles0, qg, kfm_, vtok_, causal, att2, attTg,
                      dg3, evac_pair, depth=3):
            iters = [(h, qi) for h in range(nheads) for qi in range(gt)]

            def part1(it, itpar):
                h, qi = it
                base, hc = (h % 2) * 64, h // 2
                qt = tiles0 + qi
                if causal:
                    kstart, ncols = 0, (qt + 1) * 128
                    dloc = qt * 128
                else:
                    kstart, ncols = qt * 128, S - qt * 128
                    dloc = 0
                b = 80 + 16 * itpar
                attv = att2[:, itpar, :]
                nsub = (ncols + 1023) // 1024
                tmxs = []
                scinfo = []
                for s_ in range(nsub):
                    c0 = s_ * 1024
                    c1 = min(ncols, c0 + 1024)
                    bs = attn_state["scores_par"]
                    attn_state["scores_par"] ^= 1
                    ps0 = bs * 1024
                    wmm_ = {"scalar": attn_state["lastexp"][bs],
                            "vector": attn_state["lastmax"][bs]}
                    def smm2(e, c0=c0, c1=c1, ps0=ps0, kstart=kstart,
                             base=base, hc=hc, qi=qi, dloc=dloc):
                        last = None
                        for o in range(c0, c1, 512):
                            n = min(512, c1 - o)
                            last = e.matmul(
                                PS[:, ps0 + (o - c0): ps0 + (o - c0) + n],
                                qg[base:base + 64, hc, qi * 128:(qi + 1) * 128],
                                kfm_[base:base + 64, hc, kstart + o: kstart + o + n],
                                start=True, stop=True)
                        if c0 <= dloc < c1:
                            last = e.matmul(
                                PS[:, ps0 + dloc - c0: ps0 + dloc - c0 + 128],
                                idb[:], (m_triu if causal else m_tril)[:],
                                start=False, stop=True)
                        return last
                    tsc = E.op("tensor", smm2, 1, wmm_, nm="sc")
                    tmx = E.op("vector", lambda e, ps0=ps0, n=c1 - c0, s_=s_,
                               b=b: e.reduce_max(
                                   st[:, b + 4 + s_:b + 5 + s_],
                                   PS[:, ps0: ps0 + n],
                                   axis=AX.X, negate=True),
                               1, {"tensor": tsc}, nm="mx")
                    tmxs.append(tmx)
                    scinfo.append((bs, ps0, c0, c1))
                    attn_state["lastmax"][bs] = tmx
                if nsub == 1:
                    bcol = b + 4
                    twb = tmxs[0]
                else:
                    bcol = b + 6
                    twb = E.op("vector", lambda e, b=b: e.tensor_reduce(
                        st[:, b + 6:b + 7], st[:, b + 4:b + 6], axis=AX.X,
                        op=ALU.min), 1, sw=tmxs[-1], nm="mx")
                texp = 0
                for s_, (bs, ps0, c0, c1) in enumerate(scinfo):
                    texp = E.op("scalar", lambda e, ps0=ps0, c0=c0, n=c1 - c0,
                                s_=s_, b=b, attv=attv, bcol=bcol: e.activation(
                                    attv[:, c0:c0 + n],
                                    PS[:, ps0: ps0 + n],
                                    AF.Exp, bias=st[:, bcol:bcol + 1],
                                    accum_out=st[:, b + 8 + s_:b + 9 + s_]),
                                1, wmax(A.wfirst(att2),
                                        {"vector": twb,
                                         "tensor": attn_state["att2r"][itpar]}),
                                nm="ex")
                    attn_state["lastexp"][bs] = texp
                return (b, nsub, texp)

            def emit_chain(itpar, tag0):
                b, nsub, texp = tag0
                if nsub == 2:
                    tsm = E.op("vector", lambda e, b=b: e.tensor_reduce(
                        st[:, b + 10:b + 11], st[:, b + 8:b + 10], axis=AX.X,
                        op=ALU.add), 1, {"scalar": texp}, nm="rc")
                    scol = b + 10
                else:
                    tsm = 0
                    scol = b + 8
                t2_ = E.op("vector", lambda e, scol=scol, b=b: e.reciprocal(
                    st[:, b + 12:b + 13], st[:, scol:scol + 1]), 1,
                    {"scalar": texp}, sw=tsm, nm="rc")
                return E.op("vector", lambda e, b=b, itpar=itpar:
                            e.tensor_scalar_mul(
                                dg3[:, itpar, 0:128], idb[:],
                                st[:, b + 12:b + 13]), 1,
                            wmax(A.wfirst(dg3),
                                 {"tensor": attn_state["att2r"][itpar]}),
                            sw=t2_, nm="dg")

            def part2(it, itpar, tag):
                h, qi = it
                base, hc = (h % 2) * 64, h // 2
                qt = tiles0 + qi
                if causal:
                    kstart, ncols = 0, (qt + 1) * 128
                else:
                    kstart, ncols = qt * 128, S - qt * 128
                attv = att2[:, itpar, :]
                nch = ncols // 128
                tdg = tag
                wnrm = {"vector": tdg}
                stat = dg3[:, itpar, 0:128]
                for cg in range((nch + 3) // 4):
                    ncc = min(4, nch - cg * 4)
                    pb = 4 + attn_state["tr_par"]
                    attn_state["tr_par"] = (attn_state["tr_par"] + 1) % 3
                    wtr = wmax(wnrm, attn_state["trw"][pb - 4] or {})
                    def trf(e, cg=cg, ncc=ncc, pb=pb, attv=attv, stat=stat):
                        last = None
                        for c in range(ncc):
                            last = e.matmul(
                                bank(pb)[:, c * 128:(c + 1) * 128],
                                attv[:, (cg * 4 + c) * 128:(cg * 4 + c + 1) * 128],
                                stat, start=True, stop=True)
                        return last
                    ttr = E.op("tensor", trf, 1, wtr, nm="tr")
                    kt0 = kstart // 128 + cg * 4
                    eeng = "scalar" if cg % 2 == 0 else "vector"
                    def evf(e, kt0=kt0, ncc=ncc, pb=pb, qi=qi, eeng=eeng):
                        fn = e.copy if eeng == "scalar" else e.tensor_copy
                        return fn(
                            attTg[:, kt0:kt0 + ncc, qi * 128:(qi + 1) * 128],
                            bank(pb)[:, 0:ncc * 128]
                            .rearrange("p (c q) -> p c q", q=128))
                    tev_ = E.op(eeng, evf, 1,
                                wmax(A.wfirst(attTg), {"tensor": ttr}), nm="ev")
                    attn_state["trw"][pb - 4] = {eeng: tev_}
                    attn_state["gev"][eeng] = tev_
                attn_state["att2r"][itpar] = E.cnt["tensor"]
                if qi == gt - 1:
                    ah = attn_state["av_par"]
                    aoff = 256 * ah
                    wav = wmax(dict(attn_state["gev"]),
                               attn_state["avw"][ah] or {})
                    N = gt * 128
                    def avf(e, h=h, base=base, N=N, aoff=aoff):
                        last = None
                        if causal:
                            kts = list(range(0, tiles0 + gt))
                        else:
                            kts = list(range(NT - 1, tiles0 - 1, -1))
                        for j, kt in enumerate(kts):
                            if causal:
                                off = max(0, kt - tiles0) * 128
                                nn = N - off
                            else:
                                off = 0
                                nn = (min(gt - 1, kt - tiles0) + 1) * 128
                            last = e.matmul(
                                bank(7)[base:base + 64, aoff + off:aoff + off + nn],
                                vtok_[:, kt, h * 64:(h + 1) * 64],
                                attTg[:, kt, off:off + nn],
                                start=(j == 0), stop=(j == len(kts) - 1))
                        return last
                    tav = E.op("tensor", avf, 1, wav, nm="av")
                    if h % 2 == 1:
                        evac_pair(hc, aoff, {"tensor": tav})
                        attn_state["avw"][ah] = {
                            "vector": E.cnt["vector"], "scalar": E.cnt["scalar"]}
                        attn_state["av_par"] ^= 1

            from collections import deque
            pend = deque()
            for idx, it in enumerate(iters):
                itpar = idx % depth
                tag0 = part1(it, itpar)
                pend.append([it, itpar, tag0, None])
                if len(pend) >= 2 and pend[-2][3] is None:
                    pend[-2][3] = emit_chain(pend[-2][1], pend[-2][2])
                if len(pend) > 2:
                    rec = pend.popleft()
                    part2(rec[0], rec[1], rec[3])
            while pend:
                rec = pend.popleft()
                if rec[3] is None:
                    rec[3] = emit_chain(rec[1], rec[2])
                part2(rec[0], rec[1], rec[3])


        def postln_tok(src3):
            C2 = (64, 80, 96, 112)
            tstat2 = ln_stats(src3, range(NT), C2)
            for t in range(NT):
                wapp = wmax(tstat2, {"vector": E.cnt["vector"]})
                ta = E.op("scalar", lambda e, t=t: e.activation(
                    sq_scr[:], src3[:, t, :], AF.Identity,
                    bias=st[:, C2[3] + t:C2[3] + t + 1],
                    scale=st[:, C2[2] + t:C2[2] + t + 1]), 1, wapp)
                E.op("vector", lambda e, t=t: e.tensor_add(
                    x_sb[:, t, :], x_sb[:, t, :], sq_scr[:]), 1, {"scalar": ta})

        def postln_tok_psum(ps_tok, t, wsrc):
            c0 = 64
            tsum = E.op("vector", lambda e: e.reduce_sum(
                st[:, c0:c0 + 1], ps_tok, axis=AX.X), 1, wsrc,
                sw=E.cnt["vector"])
            tsq = E.op("scalar", lambda e: e.activation(
                sq_scr[:], ps_tok, AF.Square, accum_out=st[:, c0 + 1:c0 + 2]), 1,
                wmax(wsrc, {"vector": E.cnt["vector"]}), sw=E.cnt["scalar"])
            t1 = E.op("vector", lambda e: e.tensor_scalar_mul(
                st[:, c0:c0 + 1], st[:, c0:c0 + 1], 1.0 / DM), 1,
                {"scalar": tsq}, sw=tsum)
            t2 = E.op("vector", lambda e: e.tensor_scalar_mul(
                st[:, c0 + 1:c0 + 2], st[:, c0 + 1:c0 + 2], 1.0 / DM), 1, sw=t1)
            t3 = E.op("vector", lambda e: e.scalar_tensor_tensor(
                st[:, 240:241], st[:, c0:c0 + 1], 1.0,
                st[:, c0:c0 + 1], ALU.mult, ALU.mult), 1, sw=t2)
            t4 = E.op("vector", lambda e: e.tensor_sub(
                st[:, c0 + 1:c0 + 2], st[:, c0 + 1:c0 + 2], st[:, 240:241]), 1, sw=t3)
            tl = E.op("scalar", lambda e: e.activation(
                st[:, c0 + 2:c0 + 3], st[:, c0 + 1:c0 + 2], AF.Ln,
                bias=cst[:, 0:1]), 1, {"vector": t4})
            ts2 = E.op("scalar", lambda e: e.activation(
                st[:, c0 + 2:c0 + 3], st[:, c0 + 2:c0 + 3], AF.Exp, scale=-0.5),
                1, sw=tl)
            t5 = E.op("vector", lambda e: e.scalar_tensor_tensor(
                st[:, c0 + 3:c0 + 4], st[:, c0:c0 + 1], -1.0,
                st[:, c0 + 2:c0 + 3], ALU.mult, ALU.mult), 1, {"scalar": ts2}, sw=t4)
            ta = E.op("scalar", lambda e: e.activation(
                sq_scr[:], ps_tok, AF.Identity,
                bias=st[:, c0 + 3:c0 + 4], scale=st[:, c0 + 2:c0 + 3]), 1,
                {"vector": t5}, sw=ts2)
            return E.op("vector", lambda e, t=t: e.tensor_add(
                x_sb[:, t, :], x_sb[:, t, :], sq_scr[:]), 1, {"scalar": ta})

        def emit_F0(seq):
            E.lab = f"{seq}:F0.preln"
            if cvt_state["buf"] is not None:
                while cvt_state["i"] < len(cvt_state["work"]):
                    cvt_step(4)
                A.free(cvt_state["buf"])
                cvt_state["buf"] = None
            z = A.alloc("f0z", [128, NT, DM], F32)
            tstat = ln_stats(x_sb[:], range(NT), STC)
            for t in range(NT):
                E.op("scalar", lambda e, t=t: e.activation(
                    z[:, t, :], x_sb[:, t, :], AF.Identity,
                    bias=st[:, STC[3] + t:STC[3] + t + 1],
                    scale=st[:, STC[2] + t:STC[2] + t + 1]), 1,
                    wmax(A.wfirst(z), tstat))
            for l in range(2):
                emit_F0_attn(seq, l, z)
                emit_F0_ffn(seq, l, z)
            E.lab = f"{seq}:F0.post"
            postln_tok(z)
            A.free(z)

        def emit_F0_attn(seq, l, z):
            wqkv_sb = A.alloc("f0wqkv", [128, 8, 1536], BF16)
            wo_sb = A.alloc("f0wo", [128, 4, 1024], BF16)
            qs_b = A.alloc("f0qs", [128, 64], F32)
            ks_b = A.alloc("f0ks", [128, 64], F32)
            tbl = A.alloc("f0tbl", [128, 2, 4, 64], F32)
            hq2 = [A.alloc("f0hq0", [128, 8, 129], BF16)] * 2
            f0b = None
            tffn_gp = E.cnt["gpsimd"]
            vroll = A.alloc("f0v", [128, 2, 4, 512], BF16)
            kroll = A.alloc("f0kf", [128, 4, 2, 512], BF16)
            qfm = A.alloc("f0qf", [128, 4, 512], BF16)
            from collections import deque as _dq
            f0_pend = _dq()

            def FULLW():
                return {"vector": E.cnt["vector"], "scalar": E.cnt["scalar"],
                        "gpsimd": E.cnt["gpsimd"]}
            f0s = {"sp": 0, "tp": 0, "ap": 0,
                   "b7w": {"gpsimd": E.cnt["gpsimd"]},
                   "otw": [{}, {}], "scw": None, "trw": None,
                   "atw": None, "avw": None, "at2r": [0, 0, 0, 0]}
            twq = wload(wqkv_sb[:], ap[f"f0_wqkv{l}"], w=A.wfirst(wqkv_sb), ch="w0")
            two = wload(wo_sb[:], ap[f"f0_wo{l}"], w=A.wfirst(wo_sb), ch="w0")
            tqs = E.op("sync", lambda e: e.dma_start(out=qs_b[:], in_=ap[f"f0_qs{l}"]),
                       16, w=A.wfirst(qs_b), ch="const")
            tks = E.op("sync", lambda e: e.dma_start(out=ks_b[:], in_=ap[f"f0_ks{l}"]),
                       16, w=A.wfirst(ks_b), ch="const")
            cSSQ, cSSK, cR = 192, 200, 208
            for w in range(4):
                E.lab = f"{seq}:F0{l}.qkv"
                if f0b is None:
                    f0b = {
                        "tq2": [E.cnt["vector"]] * 2, "tk2": [E.cnt["gpsimd"]] * 2,
                        "tsq": [E.cnt["scalar"]] * 2, "tkb": [E.cnt["gpsimd"]] * 2,
                        "trq": [E.cnt["vector"]] * 2, "trk": [E.cnt["vector"]] * 2,
                        "ttrq": [E.cnt["tensor"]] * 2, "ttrk": [E.cnt["tensor"]] * 2,
                        "gpphase": E.cnt["gpsimd"]}
                f0b["q32"] = [A.alloc("f0q32", [128, 512], F32)] * 2
                f0b["k32"] = [A.alloc(f"f0k32{p}", [128, 512], F32) for p in range(2)]
                f0b["kh2"] = [A.alloc("f0kh2", [128, 512], F32)] * 2
                f0b["qb"] = [A.alloc(f"f0qb{p}", [128, 512], BF16) for p in range(2)]
                f0b["kb"] = [A.alloc("f0kb", [128, 512], BF16)] * 2
                f0b["gpphase"] = max(f0b["gpphase"], tffn_gp)
                for i in range(GT):
                    tg = 4 * w + i
                    par = tg % 2
                    q32 = f0b["q32"][par]; k32 = f0b["k32"][par]
                    kh2 = f0b["kh2"][par]; qb = f0b["qb"][par]
                    kb = f0b["kb"][par]
                    tblv = tbl[:, par, :, :]
                    wt_ = wmax(A.wfirst(tbl), {"vector": f0b["tq2"][par],
                                               "gpsimd": f0b["tk2"][par]})
                    for ti_, nm in enumerate(("xpos_csq", "xpos_ssq", "xpos_csk", "xpos_ssk")):
                        E.op("sync", lambda e, ti_=ti_, nm=nm, tg=tg,
                             tblv=tblv: e.dma_start(
                                 out=tblv[:, ti_, :], in_=ap[nm][:, tg, :]),
                            16, wt_, ch="tab")
                    ttab = E.cnt["tab"]
                    wz = {"vector": E.cnt["vector"], "tensor": E.cnt["tensor"]}
                    hq = hq2[par]
                    if tg == 0:
                        tc = E.op("vector", lambda e, hq=hq: e.memset(
                            hq[:, :, 0:1], 0.0), 1, wmax(A.wfirst(hq2[0]), wz))
                    else:
                        hqp = hq2[par ^ 1]
                        tc = E.op("vector", lambda e, hq=hq, hqp=hqp: e.tensor_copy(
                            hq[:, :, 0:1], hqp[:, :, 128:129]), 1,
                            wmax(A.wfirst(hq2[1]), wz), sw=E.cnt["vector"])
                    hb = hbf[:, tg % 2, :]
                    ta = E.op("scalar", lambda e, tg=tg, hb=hb: e.copy(
                        hb, z[:, tg, :]), 1,
                        {"vector": E.cnt["vector"], "tensor": E.cnt["tensor"],
                         "gpsimd": f0b["gpphase"]},
                        sw=E.cnt["scalar"])
                    pb = 4 + (tg % 2)
                    wtr = {"scalar": ta, "vector": E.cnt["vector"]}
                    for c in range(NC):
                        E.op("tensor", lambda e, c=c, pb=pb, hb=hb: e.transpose(
                            bankb(pb)[:, c * 128:(c + 1) * 128],
                            hb[:, c * 128:(c + 1) * 128], idb[:]), 1, wtr)
                    tt = E.tok()
                    thq = E.op("vector", lambda e, pb=pb, hq=hq: e.tensor_copy(
                        hq[:, :, 1:129],
                        bankb(pb)[:, 0:1024].rearrange("p (c q) -> p c q", q=128)),
                        1, tt, sw=tc)
                    def qkvmm(e, cols0, bk, hq=hq):
                        last = None
                        for kc in range(8):
                            off = 1 if kc < 4 else 0
                            last = e.matmul(
                                bank(bk), hq[:, kc, off:off + 128],
                                wqkv_sb[:, kc, cols0:cols0 + 512],
                                start=(kc == 0), stop=(kc == 7))
                        return last
                    wmm = {"vector": thq, "w0": twq, "scalar": E.cnt["scalar"],
                           "gpsimd": E.cnt["gpsimd"]}
                    tmq = E.op("tensor", lambda e: qkvmm(e, 0, 0), 1, wmm)
                    tmk = E.op("tensor", lambda e: qkvmm(e, 512, 1), 1)
                    tmv = E.op("tensor", lambda e: qkvmm(e, 1024, 2), 1)
                    teq = E.op("vector", lambda e, q32=q32: e.tensor_copy(
                        q32[:], bank(0)),
                        1, wmax(A.wfirst(q32), {"tensor": tmq,
                                                "scalar": max(f0b["tsq"])}))
                    tek = E.op("vector", lambda e, k32=k32: e.tensor_copy(
                        k32[:], bank(1)),
                        1, wmax(A.wfirst(k32), {"tensor": tmk,
                                                "gpsimd": f0b["tk2"][par]}))
                    E.op("vector", lambda e, w=w, i=i: e.tensor_copy(
                        vroll[:, w % 2, i, :], bank(2)), 1,
                        wmax(A.wfirst(vroll), {"tensor": tmv}))
                    tsq_ = E.op("scalar", lambda e, q32=q32: e.activation(
                        sq_scr[:, 0:512], q32[:], AF.Square), 1,
                        {"vector": teq, "gpsimd": f0b["gpphase"]})
                    f0b["tsq"][par] = tsq_
                    trq = E.op("vector", lambda e: e.tensor_reduce(
                        st[:, cSSQ:cSSQ + 8],
                        sq_scr[:, 0:512].rearrange("p (h d) -> p h d", d=64),
                        axis=AX.X, op=ALU.add), 1, {"scalar": tsq_})
                    f0b["trq"][par] = trq
                    tks_ = E.op("gpsimd", lambda e, kh2=kh2, k32=k32: e.tensor_mul(
                        kh2[:], k32[:], k32[:]), 1,
                        wmax(A.wfirst(kh2), {"vector": max(
                            tek, f0b["trk"][0], f0b["trk"][1])}))
                    trk = E.op("vector", lambda e, kh2=kh2: e.tensor_reduce(
                        st[:, cSSK:cSSK + 8],
                        kh2[:].rearrange("p (h d) -> p h d", d=64),
                        axis=AX.X, op=ALU.add), 1, {"gpsimd": tks_})
                    f0b["trk"][par] = trk
                    tln = E.op("scalar", lambda e: e.activation(
                        st[:, cR:cR + 16], st[:, cSSQ:cSSQ + 16], AF.Ln), 1,
                        {"vector": trk})
                    tex = E.op("scalar", lambda e: e.activation(
                        st[:, cR:cR + 16], st[:, cR:cR + 16], AF.Exp, scale=-0.5),
                        1, sw=tln)
                    tmn = E.op("vector", lambda e: e.tensor_scalar_min(
                        st[:, cR:cR + 16], st[:, cR:cR + 16], 1.0e12), 1,
                        {"scalar": tex})
                    rqv = st[:, cR:cR + 8].rearrange("p (h o) -> p h o", o=1)
                    qhv = q32[:].rearrange("p (h d) -> p h d", d=64)
                    E.op("vector", lambda e, qhv=qhv: e.tensor_tensor(
                        qhv, qhv,
                        rqv.broadcast_to([128, 8, 64]), ALU.mult), 1,
                        {"vector": tmn}, sw=tmn)
                    E.op("vector", lambda e, qhv=qhv: e.tensor_tensor(
                        qhv, qhv,
                        qs_b[:].unsqueeze(1).broadcast_to([128, 8, 64]), ALU.mult),
                        1, {"const": tqs}, sw=E.cnt["vector"])
                    s2v = sq_scr[:, 512:1024].rearrange("p (h d) -> p h d", d=64)
                    E.op("vector", lambda e, qhv=qhv, tblv=tblv: e.tensor_tensor(
                        s2v[:, :, 0:32], qhv[:, :, 32:64],
                        tblv[:, 1, 0:32].unsqueeze(1).broadcast_to([128, 8, 32]),
                        ALU.mult), 1, {"tab": ttab}, sw=E.cnt["vector"])
                    E.op("vector", lambda e, qhv=qhv, tblv=tblv: e.tensor_tensor(
                        s2v[:, :, 32:64], qhv[:, :, 0:32],
                        tblv[:, 1, 32:64].unsqueeze(1).broadcast_to([128, 8, 32]),
                        ALU.mult), 1)
                    tq2 = E.op("vector", lambda e, qhv=qhv, tblv=tblv: e.tensor_tensor(
                        qhv, qhv,
                        tblv[:, 0, :].unsqueeze(1).broadcast_to([128, 8, 64]),
                        ALU.mult), 1, sw=E.cnt["vector"])
                    f0b["tq2"][par] = tq2
                    tqb = E.op("vector", lambda e, qb=qb, q32=q32: e.tensor_add(
                        qb[:], q32[:], sq_scr[:, 512:1024]), 1,
                        wmax(A.wfirst(qb), {"tensor": f0b["ttrq"][par]}), sw=tq2)
                    rkv = st[:, cR + 8:cR + 16].rearrange("p (h o) -> p h o", o=1)
                    khv = k32[:].rearrange("p (h d) -> p h d", d=64)
                    k2v = kh2[:].rearrange("p (h d) -> p h d", d=64)
                    E.op("gpsimd", lambda e, khv=khv: e.tensor_tensor(
                        khv, khv,
                        rkv.broadcast_to([128, 8, 64]), ALU.mult), 1,
                        {"vector": tmn}, sw=0)
                    E.op("gpsimd", lambda e, khv=khv: e.tensor_tensor(
                        khv, khv,
                        ks_b[:].unsqueeze(1).broadcast_to([128, 8, 64]), ALU.mult),
                        1, {"const": tks}, sw=E.cnt["gpsimd"])
                    E.op("gpsimd", lambda e, khv=khv, k2v=k2v, tblv=tblv: e.tensor_tensor(
                        k2v[:, :, 0:32], khv[:, :, 32:64],
                        tblv[:, 3, 0:32].unsqueeze(1).broadcast_to([128, 8, 32]),
                        ALU.mult), 1, wmax(A.wfirst(kh2), {"tab": ttab, "vector": trk}),
                        sw=E.cnt["gpsimd"])
                    E.op("gpsimd", lambda e, khv=khv, k2v=k2v, tblv=tblv: e.tensor_tensor(
                        k2v[:, :, 32:64], khv[:, :, 0:32],
                        tblv[:, 3, 32:64].unsqueeze(1).broadcast_to([128, 8, 32]),
                        ALU.mult), 1)
                    tk2 = E.op("gpsimd", lambda e, khv=khv, tblv=tblv: e.tensor_tensor(
                        khv, khv,
                        tblv[:, 2, :].unsqueeze(1).broadcast_to([128, 8, 64]),
                        ALU.mult), 1, sw=E.cnt["gpsimd"])
                    f0b["tk2"][par] = tk2
                    tkb = E.op("gpsimd", lambda e, kb=kb, k32=k32, kh2=kh2: e.tensor_add(
                        kb[:], k32[:], kh2[:]), 1,
                        wmax(A.wfirst(kb), {"tensor": max(f0b["ttrk"])}), sw=tk2)
                    f0b["tkb"][par] = tkb
                    pb2 = 4 + (i % 2)
                    wtr2 = {"vector": tqb, "scalar": E.cnt["scalar"]}
                    for c in range(4):
                        E.op("tensor", lambda e, c=c, pb2=pb2, qb=qb: e.transpose(
                            bankb(pb2)[:, c * 128:(c + 1) * 128],
                            qb[:, c * 128:(c + 1) * 128], idb[:]), 1, wtr2)
                    f0b["ttrq"][par] = E.cnt["tensor"]
                    tt2 = E.tok()
                    E.op("vector", lambda e, i=i, pb2=pb2: e.tensor_copy(
                        qfm[:, :, i * 128:(i + 1) * 128],
                        bankb(pb2)[:, 0:512].rearrange("p (c q) -> p c q", q=128)),
                        1, wmax(A.wfirst(qfm), tt2))
                    for c in range(4):
                        E.op("tensor", lambda e, c=c, pb2=pb2, kb=kb: e.transpose(
                            bankb(pb2)[:, 512 + c * 128: 512 + (c + 1) * 128],
                            kb[:, c * 128:(c + 1) * 128], idb[:]), 1,
                            {"vector": E.cnt["vector"], "gpsimd": tkb})
                    f0b["ttrk"][par] = E.cnt["tensor"]
                    tt3 = E.tok()
                    E.op("vector", lambda e, i=i, w=w, pb2=pb2: e.tensor_copy(
                        kroll[:, :, w % 2, i * 128:(i + 1) * 128],
                        bankb(pb2)[:, 512:1024].rearrange("p (c q) -> p c q", q=128)),
                        1, wmax(A.wfirst(kroll), tt3))
                for b_ in (f0b["q32"][0], f0b["k32"][0], f0b["k32"][1],
                           f0b["qb"][0], f0b["qb"][1], f0b["kb"][0],
                           f0b["kh2"][0]):
                    A.free(b_)
                E.lab = f"{seq}:F0{l}.att"
                att2f = A.alloc("f0att", [128, 4, 640], BF16)
                attT2 = A.alloc("f0attT", [128, 2, 5, 128], BF16)
                otile2 = A.alloc("f0o", [128, 2, 4, 128], BF16)
                dgf = A.alloc("f0dg", [128, 4, 128], BF16)
                f0s["scw"] = [dict(FULLW()), dict(FULLW())]
                f0s["trw"] = [dict(FULLW()), dict(FULLW())]
                f0s["atw"] = [{}, {}]
                f0s["avw"] = [dict(FULLW()), dict(FULLW())]

                def f0_part1(it, itpar, w=w):
                    qt, h = it
                    base, hc = (h % 2) * 64, h // 2
                    attv = att2f[:, itpar, :]
                    b = 80 + 8 * itpar
                    bs = f0s["sp"]
                    f0s["sp"] ^= 1
                    ps0 = bs * 1024
                    wsc = f0s["scw"][bs]
                    if w == 0:
                        ncols = (qt + 1) * 128
                        def smm0(e, qt=qt, base=base, hc=hc, ncols=ncols,
                                 ps0=ps0):
                            e.matmul(
                                PS[:, ps0:ps0 + ncols],
                                qfm[base:base + 64, hc, qt * 128:(qt + 1) * 128],
                                kroll[base:base + 64, hc, 0, 0:ncols],
                                start=True, stop=True)
                            return e.matmul(
                                PS[:, ps0 + qt * 128:ps0 + (qt + 1) * 128],
                                idb[:], m_triu[:], start=False, stop=True)
                        tsc = E.op("tensor", smm0, 1, wsc)
                    else:
                        n1 = 512 - qt * 128
                        n2 = qt * 128 + 128
                        ncols = 640
                        sA, sB = (w - 1) % 2, w % 2
                        def smmw(e, qt=qt, base=base, hc=hc, n1=n1, n2=n2,
                                 sA=sA, sB=sB, ps0=ps0):
                            e.matmul(PS[:, ps0:ps0 + n1],
                                     qfm[base:base + 64, hc, qt * 128:(qt + 1) * 128],
                                     kroll[base:base + 64, hc, sA, qt * 128:512],
                                     start=True, stop=True)
                            if n1 < 512:
                                e.matmul(PS[:, ps0 + n1:ps0 + 512],
                                         qfm[base:base + 64, hc, qt * 128:(qt + 1) * 128],
                                         kroll[base:base + 64, hc, sB, 0:512 - n1],
                                         start=True, stop=True)
                            e.matmul(
                                PS[:, ps0 + 512:ps0 + 640],
                                qfm[base:base + 64, hc, qt * 128:(qt + 1) * 128],
                                kroll[base:base + 64, hc, sB, 512 - n1:n2],
                                start=True, stop=True)
                            e.matmul(PS[:, ps0:ps0 + 128], idb[:], m_tril[:],
                                     start=False, stop=True)
                            return e.matmul(
                                PS[:, ps0 + 512:ps0 + 640], idb[:], m_triu[:],
                                start=False, stop=True)
                        tsc = E.op("tensor", smmw, 1, wsc)
                    tmx = E.op("vector", lambda e, ncols=ncols, ps0=ps0, b=b:
                               e.reduce_max(
                                   st[:, b:b + 1], PS[:, ps0:ps0 + ncols],
                                   axis=AX.X, negate=True), 1, {"tensor": tsc},
                               sw=E.cnt["vector"])
                    wexp = wmax(A.wfirst(att2f),
                                {"vector": tmx,
                                 "tensor": f0s["at2r"][itpar]})
                    texp = E.op("scalar", lambda e, n=ncols, b=b,
                                ps0=ps0, attv=attv: e.activation(
                                    attv[:, 0:n], PS[:, ps0:ps0 + n], AF.Exp,
                                    bias=st[:, b:b + 1],
                                    accum_out=st[:, b + 1:b + 2]), 1, wexp)
                    f0s["scw"][bs] = {"scalar": texp, "vector": tmx}
                    return (b, texp)

                def f0_chain(itpar, tag0):
                    b, texp = tag0
                    trc = E.op("vector", lambda e, b=b: e.reciprocal(
                        st[:, b + 4:b + 5], st[:, b + 1:b + 2]), 1,
                        {"scalar": texp})
                    return E.op("vector", lambda e, b=b, itpar=itpar:
                                e.tensor_scalar_mul(
                                    dgf[:, itpar, :], idb[:],
                                    st[:, b + 4:b + 5]), 1,
                                wmax(A.wfirst(dgf),
                                     {"tensor": f0s["at2r"][itpar]}), sw=trc)

                def f0_part2(it, itpar, tnrm, w=w):
                    qt, h = it
                    base, hc = (h % 2) * 64, h // 2
                    attv = att2f[:, itpar, :]
                    nch = (qt + 1) if w == 0 else 5
                    sl = (qt * F0_HEADS + h) % 2
                    tev = 0
                    for cg in range((nch + 3) // 4):
                        ncc = min(4, nch - cg * 4)
                        pb3 = 4 + f0s["tp"]
                        f0s["tp"] ^= 1
                        wtr3 = wmax({"vector": tnrm}, f0s["trw"][pb3 - 4])
                        def trf(e, cg=cg, ncc=ncc, pb3=pb3, attv=attv,
                                itpar=itpar):
                            last = None
                            for c in range(ncc):
                                last = e.matmul(
                                    bank(pb3)[:, c * 128:(c + 1) * 128],
                                    attv[:, (cg * 4 + c) * 128:(cg * 4 + c + 1) * 128],
                                    dgf[:, itpar, :], start=True, stop=True)
                            return last
                        ttr = E.op("tensor", trf, 1, wtr3)
                        tev = E.op("scalar", lambda e, ncc=ncc, pb3=pb3, sl=sl,
                                   cg=cg: e.copy(
                                       attT2[:, sl, cg * 4:cg * 4 + ncc, :]
                                       .rearrange("p c q -> p (c q)"),
                                       bank(pb3)[:, 0:ncc * 128]), 1,
                                   wmax(A.wfirst(attT2),
                                        {"tensor": ttr}, f0s["atw"][sl]))
                        f0s["trw"][pb3 - 4] = {"scalar": tev}
                    f0s["at2r"][itpar] = E.cnt["tensor"]
                    ap_ = f0s["ap"]
                    pcol = ap_ * 128
                    wav = wmax({"scalar": tev}, f0s["avw"][ap_])
                    def avf(e, w=w, qt=qt, h=h, base=base, nch=nch, sl=sl,
                            pcol=pcol):
                        last = None
                        for c in range(nch):
                            if w == 0:
                                vsl, ti = 0, c
                            else:
                                wt2 = qt + c
                                vsl = (w - 1 + wt2 // 4) % 2
                                ti = wt2 % 4
                            last = e.matmul(
                                bank(6)[base:base + 64, pcol:pcol + 128],
                                vroll[:, vsl, ti, h * 64:(h + 1) * 64],
                                attT2[:, sl, c, :], start=(c == 0),
                                stop=(c == nch - 1))
                        return last
                    tav = E.op("tensor", avf, 1, wav)
                    f0s["atw"][sl] = {"tensor": tav}
                    if h % 2 == 1:
                        tevo = E.op("vector", lambda e, hc=hc, qt=qt,
                                    pcol=pcol: e.tensor_copy(
                                        otile2[:, qt % 2, hc, :],
                                        bank(6)[:, pcol:pcol + 128]), 1,
                                    wmax(A.wfirst(otile2), {"tensor": tav},
                                         f0s["otw"][qt % 2]))
                        f0s["avw"][ap_] = {"vector": tevo}
                        f0s["ap"] ^= 1
                    if h == F0_HEADS - 1:
                        tz = 0
                        for half in range(2):
                            def womm(e, qt=qt, half=half):
                                last = None
                                for ic in range(4):
                                    last = e.matmul(
                                        PS[:, 3584:4096],
                                        otile2[:, qt % 2, ic, :],
                                        wo_sb[:, ic, half * 512:half * 512 + 512],
                                        start=(ic == 0), stop=(ic == 3))
                                return last
                            two_ = E.op("tensor", womm, 1,
                                        wmax({"vector": E.cnt["vector"],
                                              "w0": two}, f0s["b7w"],
                                             ({"vector": tz} if half else {})))
                            tz = E.op("vector", lambda e, w=w, qt=qt,
                                      half=half: e.tensor_tensor(
                                          z[:, 4 * w + qt, half * 512:half * 512 + 512],
                                          z[:, 4 * w + qt, half * 512:half * 512 + 512],
                                          PS[:, 3584:4096], ALU.add), 1,
                                      {"tensor": two_})
                        f0s["b7w"] = {"vector": tz}
                        f0s["otw"][qt % 2] = {"tensor": two_}

                f0_iters = [(qt, h) for qt in range(GT)
                            for h in range(F0_HEADS)]
                f0_pend.clear()
                for idx, it in enumerate(f0_iters):
                    itpar = idx % 4
                    tag0 = f0_part1(it, itpar)
                    f0_pend.append([it, itpar, tag0, None])
                    if len(f0_pend) >= 2 and f0_pend[-2][3] is None:
                        f0_pend[-2][3] = f0_chain(f0_pend[-2][1], f0_pend[-2][2])
                    if len(f0_pend) > 2:
                        rec = f0_pend.popleft()
                        f0_part2(rec[0], rec[1], rec[3])
                while f0_pend:
                    rec = f0_pend.popleft()
                    if rec[3] is None:
                        rec[3] = f0_chain(rec[1], rec[2])
                    f0_part2(rec[0], rec[1], rec[3])
                for b_ in (att2f, attT2, otile2, dgf):
                    A.free(b_)
            for b_ in (wqkv_sb, wo_sb, qs_b, ks_b, tbl, hq2[0],
                       vroll, kroll, qfm):
                A.free(b_)


        def emit_F0_ffn(seq, l, z):
            E.lab = f"{seq}:F0{l}.stats"
            h3 = A.alloc("f0h3", [128, F0_KC2, 1024], BF16)
            b1sb_ = None
            h2 = A.alloc("f0h2", [128, 8, 1025], BF16)

            b1sb = A.alloc("f0b1", [128, 2 * F0_KC2], F32)
            b2h = A.alloc("f0b2", [128, 512], BF16)
            lngp = A.alloc("f0lngp", [128, 8], F32)
            lnbp = A.alloc("f0lnbp", [128, 8], F32)
            E.op("sync", lambda e: e.dma_start(out=b1sb[:], in_=ap[f"f0_b1{l}"]),
                 16, w=A.wfirst(b1sb), ch="const")

            E.op("sync", lambda e: e.dma_start(out=lngp[:], in_=ap[f"f0_lngp{l}"]),
                 16, w=A.wfirst(lngp), ch="const")
            E.op("sync", lambda e: e.dma_start(out=lnbp[:], in_=ap[f"f0_lnbp{l}"]),
                 16, w=A.wfirst(lnbp), ch="const")
            tcn = E.cnt["const"]
            cS1, cS2, cQ1, cQ2 = 0, 16, 32, 48
            cS2s, cQ2s, cSH, cQH = 64, 80, 96, 112
            cRH, cNH, cRF, cNF = 128, 144, 160, 176
            stv = st[:].rearrange("p (a b) -> p a b", b=16)
            tk0 = E.tok()
            for t in range(NT):
                E.op("scalar", lambda e, t=t: e.activation(
                    sq_scr[:, 0:512], z[:, t, 0:512], AF.Square,
                    accum_out=st[:, cQ1 + t:cQ1 + t + 1]), 1, tk0,
                    sw=E.cnt["scalar"])
                E.op("scalar", lambda e, t=t: e.activation(
                    sq_scr[:, 512:1024], z[:, t, 512:1024], AF.Square,
                    accum_out=st[:, cQ2 + t:cQ2 + t + 1]), 1)
                E.op("vector", lambda e, t=t: e.tensor_reduce(
                    stv[:, 0:2, t], z[:, t, :].rearrange("p (a b) -> p a b", b=512),
                    axis=AX.X, op=ALU.add), 1, tk0)
            tst = {"scalar": E.cnt["scalar"], "vector": E.cnt["vector"]}
            for csrc, cdst in ((cS2, cS2s), (cQ2, cQ2s)):
                E.op("sync", lambda e, csrc=csrc, cdst=cdst: e.dma_start(
                    out=st[1:128, cdst:cdst + 16], in_=st[0:127, csrc:csrc + 16]),
                    16, tst, ch="tab")
                E.op("sync", lambda e, csrc=csrc, cdst=cdst: e.dma_start(
                    out=st[0:1, cdst + 1:cdst + 16], in_=st[127:128, csrc:csrc + 15]),
                    16, ch="tab")
            tz0 = E.op("vector", lambda e: e.memset(st[:, 240:242], 0.0), 1, tst)
            E.op("sync", lambda e: e.dma_start(
                out=st[0:1, cS2s:cS2s + 1], in_=st[0:1, 240:241]), 16,
                {"vector": tz0}, ch="tab")
            E.op("sync", lambda e: e.dma_start(
                out=st[0:1, cQ2s:cQ2s + 1], in_=st[0:1, 241:242]), 16, ch="tab")
            tsh = E.cnt["tab"]
            t1 = E.op("vector", lambda e: e.tensor_add(
                st[:, cSH:cSH + 16], st[:, cS1:cS1 + 16], st[:, cS2s:cS2s + 16]),
                1, wmax(tst, {"tab": tsh}), sw=tz0)
            t2 = E.op("vector", lambda e: e.tensor_add(
                st[:, cQH:cQH + 16], st[:, cQ1:cQ1 + 16], st[:, cQ2s:cQ2s + 16]), 1)
            t3 = E.op("vector", lambda e: e.tensor_scalar_mul(
                st[:, cSH:cSH + 16], st[:, cSH:cSH + 16], 1.0 / DM), 1, sw=t1)
            t4 = E.op("vector", lambda e: e.tensor_scalar_mul(
                st[:, cQH:cQH + 16], st[:, cQH:cQH + 16], 1.0 / DM), 1, sw=t2)
            t5 = E.op("vector", lambda e: e.scalar_tensor_tensor(
                st[:, 224:240], st[:, cSH:cSH + 16], 1.0,
                st[:, cSH:cSH + 16], ALU.mult, ALU.mult), 1, sw=t3)
            t6 = E.op("vector", lambda e: e.tensor_sub(
                st[:, cQH:cQH + 16], st[:, cQH:cQH + 16], st[:, 224:240]), 1, sw=t5)
            t7 = E.op("scalar", lambda e: e.activation(
                st[:, cRH:cRH + 16], st[:, cQH:cQH + 16], AF.Ln, bias=cst[:, 0:1]),
                1, {"vector": t6})
            t8 = E.op("scalar", lambda e: e.activation(
                st[:, cRH:cRH + 16], st[:, cRH:cRH + 16], AF.Exp, scale=-0.5),
                1, sw=t7)
            t9 = E.op("vector", lambda e: e.scalar_tensor_tensor(
                st[:, cNH:cNH + 16], st[:, cSH:cSH + 16], -1.0,
                st[:, cRH:cRH + 16], ALU.mult, ALU.mult), 1, {"scalar": t8}, sw=t6)
            wfw = {"vector": t9, "scalar": t8}
            for csrc, cdst in ((cRH, cRF), (cNH, cNF)):
                E.op("sync", lambda e, csrc=csrc, cdst=cdst: e.dma_start(
                    out=st[0:127, cdst:cdst + 16], in_=st[1:128, csrc:csrc + 16]),
                    16, wfw, ch="tab")
                E.op("sync", lambda e, csrc=csrc, cdst=cdst: e.dma_start(
                    out=st[127:128, cdst:cdst + 15], in_=st[0:1, csrc + 1:csrc + 16]),
                    16, ch="tab")
            tz1 = E.op("vector", lambda e: e.memset(st[:, 240:241], 1.0), 1, wfw)
            tz2 = E.op("vector", lambda e: e.memset(st[:, 241:242], 0.0), 1)
            E.op("sync", lambda e: e.dma_start(
                out=st[127:128, cRF + 15:cRF + 16], in_=st[127:128, 240:241]),
                16, {"vector": tz2}, ch="tab")
            E.op("sync", lambda e: e.dma_start(
                out=st[127:128, cNF + 15:cNF + 16], in_=st[127:128, 241:242]),
                16, ch="tab")
            tshf = E.cnt["tab"]
            ton = E.op("tensor", lambda e: e.matmul(
                bank(7)[:, 0:1], cst[0:1, 4:132], st[0:1, cNH:cNH + 1],
                start=True, stop=True), 1,
                wmax({"tab": tshf}, {"vector": E.cnt["vector"],
                                     "scalar": E.cnt["scalar"]}))
            for c in range(4, 8):
                E.op("vector", lambda e, c=c: e.tensor_scalar(
                    h2[:, c, 0:1], bank(7)[:, 0:1],
                    lngp[:, c:c + 1], lnbp[:, c:c + 1],
                    ALU.mult, ALU.add), 1,
                    wmax(A.wfirst(h2), {"tensor": ton, "const": tcn}))
            for g in range(2):
                g0 = g * 8
                if g == 1:
                    E.op("vector", lambda e: e.tensor_copy(
                        h2[:, 4:8, 0:1], h2[:, 4:8, 1024:1025]), 1,
                        {"tensor": E.cnt["tensor"]}, sw=E.cnt["vector"])
                E.lab = f"{seq}:F0{l}.h2"
                for i in range(8):
                    t = g0 + i
                    wap = {"tab": tshf, "vector": E.cnt["vector"],
                           "tensor": E.cnt["tensor"]}
                    hb = hbf[:, t % 2, :]
                    E.op("scalar", lambda e, t=t, hb=hb: e.activation(
                        hb[:, 0:512], z[:, t, 0:512], AF.Identity,
                        bias=st[:, cNH + t:cNH + t + 1],
                        scale=st[:, cRH + t:cRH + t + 1]), 1, wap)
                    ta2 = E.op("scalar", lambda e, t=t, hb=hb: e.activation(
                        hb[:, 512:1024], z[:, t, 512:1024], AF.Identity,
                        bias=st[:, cNF + t:cNF + t + 1],
                        scale=st[:, cRF + t:cRF + t + 1]), 1)
                    pb = 4 + (i % 2)
                    wtr = {"scalar": ta2, "vector": E.cnt["vector"]}
                    for c in range(NC):
                        E.op("tensor", lambda e, c=c, pb=pb, hb=hb: e.transpose(
                            bankb(pb)[:, c * 128:(c + 1) * 128],
                            hb[:, c * 128:(c + 1) * 128], idb[:]), 1, wtr)
                    tt = E.tok()
                    for c in range(NC):
                        E.op("vector", lambda e, i=i, pb=pb, c=c: e.tensor_scalar(
                            h2[:, c, 1 + i * 128: 129 + i * 128],
                            bankb(pb)[:, c * 128:(c + 1) * 128],
                            lngp[:, c:c + 1], lnbp[:, c:c + 1],
                            ALU.mult, ALU.add), 1, wmax(tt, {"const": tcn}))
                th2 = E.tok()
                E.lab = f"{seq}:F0{l}.w1"
                w1sl = A.alloc("f0w1s", [128, 4, 8, 128], BF16)
                tw1 = [0, 0]
                tst_pair = [E.cnt["vector"]] * 4
                tgl_pair = [E.cnt["scalar"]] * 4
                hbfv = hbf[:].rearrange("p a b -> p (a b)")
                w1bv = wcvt[f"w1b{l}"].rearrange("p (a b) -> p a b", b=5460)
                for j in range(F0_KC2):
                    cj = 128 if j < F0_KC2 - 1 else F0_LASTC
                    sl = j % 2
                    chn = "wC" if sl == 0 else "wD"
                    weng = "sync" if sl == 0 else "gpsimd"
                    wld_ = wmax(A.wfirst(w1sl), {"tensor": tw1[sl],
                                                 "cv": 1, "cva": 1})
                    twa = wload(w1sl[:, sl * 2, :, 0:cj],
                                w1bv[:, :, 128 * j:128 * j + cj],
                                w=wld_, ch=chn, eng=weng)
                    twg = wload(w1sl[:, sl * 2 + 1, :, 0:cj],
                                w1bv[:, :, F0_HID + 128 * j:F0_HID + 128 * j + cj],
                                ch=chn, eng=weng)
                    for n in range(2):
                        pid = (2 * j + n) % 4
                        bka, bkg = 2 * pid, 2 * pid + 1
                        wm1 = wmax(th2, {chn: twg, "vector": tst_pair[pid]})
                        def uamm(e, j=j, cj=cj, n=n, bka=bka, sl=sl):
                            last = None
                            for kc in range(8):
                                off = 1 if kc < 4 else 0
                                last = e.matmul(
                                    bank(bka)[0:cj, 0:512],
                                    w1sl[:, sl * 2, kc, 0:cj],
                                    h2[:, kc, off + n * 512: off + n * 512 + 512],
                                    start=(kc == 0), stop=(kc == 7))
                            return last
                        tua = E.op("tensor", uamm, 1, wm1)
                        def ugmm(e, j=j, cj=cj, n=n, bkg=bkg, sl=sl):
                            last = None
                            for kc in range(8):
                                off = 1 if kc < 4 else 0
                                last = e.matmul(
                                    bank(bkg)[0:cj, 0:512],
                                    w1sl[:, sl * 2 + 1, kc, 0:cj],
                                    h2[:, kc, off + n * 512: off + n * 512 + 512],
                                    start=(kc == 0), stop=(kc == 7))
                            return last
                        tug = E.op("tensor", ugmm, 1, {"scalar": tgl_pair[pid]})
                        tgl = E.op("scalar", lambda e, cj=cj, j=j, bkg=bkg,
                                   pid=pid: e.activation(
                            hbfv[0:cj, pid * 512:pid * 512 + 512],
                            bank(bkg)[0:cj, 0:512], AF.Gelu,
                            bias=b1sb[0:cj, F0_KC2 + j:F0_KC2 + j + 1]), 1,
                            {"tensor": tug, "vector": tst_pair[pid],
                             "const": tcn})
                        tst_ = E.op("vector", lambda e, cj=cj, j=j, n=n,
                                    bka=bka, pid=pid: e.scalar_tensor_tensor(
                            h3[0:cj, j, n * 512:n * 512 + 512],
                            bank(bka)[0:cj, 0:512], b1sb[0:cj, j:j + 1],
                            hbfv[0:cj, pid * 512:pid * 512 + 512],
                            ALU.add, ALU.mult), 1,
                            wmax(A.wfirst(h3), {"scalar": tgl}))
                        tst_pair[pid] = tst_
                        tgl_pair[pid] = tgl
                        tw1[sl] = E.cnt["tensor"]
                th3 = E.tok()
                A.free(w1sl)
                E.lab = f"{seq}:F0{l}.w2"
                w2sl = A.alloc("f0w2s", [128, 8, 512], BF16)
                tw2 = [0] * 8
                bank_free = {}
                tz_par = [E.cnt["gpsimd"]] * 2
                for half in range(2):
                    tb2h = E.op("gpsimd", lambda e, half=half: e.dma_start(
                        out=b2h[:], in_=ap[f"f0_b2{l}"][:, half * 512:half * 512 + 512]),
                        16, wmax(A.wfirst(b2h), {"vector": E.cnt["vector"]}),
                        ch="w1")
                    w2bv = wcvt[f"w2b{l}"].rearrange("p (a b) -> p a b", b=1024)
                    for kc in range(F0_KC2):
                        cj = 128 if kc < F0_KC2 - 1 else F0_LASTC
                        sl = kc % 8
                        chn = "wA" if kc % 2 == 0 else "wB"
                        weng = "sync" if kc % 2 == 0 else "gpsimd"
                        tw_ = wload(w2sl[:, sl, :],
                                    w2bv[:, kc, half * 512:half * 512 + 512],
                                    w=wmax(A.wfirst(w2sl), {"tensor": tw2[sl],
                                                            "cv": 1, "cva": 1}),
                                    ch=chn, eng=weng)
                        wv_ = wmax(th3, {chn: tw_})
                        if kc == 0:
                            wv_ = wmax(wv_, bank_free)
                        def w2mm(e, kc=kc, cj=cj, sl=sl):
                            last = None
                            for ti in range(8):
                                last = e.matmul(
                                    bank(ti), h3[0:cj, kc, ti * 128:(ti + 1) * 128],
                                    w2sl[0:cj, sl, :],
                                    start=(kc == 0), stop=(kc == F0_KC2 - 1))
                            return last
                        tw2[sl] = E.op("tensor", w2mm, 1, wv_)
                    tmm2 = E.cnt["tensor"]
                    for ti in range(8):
                        so = 512 * (ti % 2)
                        tb_ = E.op("vector", lambda e, ti=ti, so=so: e.tensor_tensor(
                            sq_scr[:, so:so + 512], bank(ti),
                            b2h[:], ALU.add), 1,
                            {"tensor": tmm2, "w1": tb2h,
                             "gpsimd": tz_par[ti % 2]})
                        tz_par[ti % 2] = E.op(
                            "gpsimd", lambda e, ti=ti, half=half, g0=g0, so=so:
                            e.tensor_tensor(
                                z[:, g0 + ti, half * 512:half * 512 + 512],
                                z[:, g0 + ti, half * 512:half * 512 + 512],
                                sq_scr[:, so:so + 512], ALU.add), 1,
                            {"vector": tb_})
                    bank_free = {"vector": E.cnt["vector"]}
                A.free(w2sl)
            for b_ in (h2, h3, b1sb, b2h, lngp, lnbp):
                A.free(b_)

        def emit_A1(seq):
            E.lab = f"{seq}:A1.h"
            idf = A.alloc("idf", [128, 128], F32)
            E.op("sync", lambda e: e.dma_start(out=idf[:], in_=ap["idf"]), 16,
                 w=A.wfirst(idf), ch="const")
            postln_idf[0] = idf
            tstat = ln_stats(x_sb[:], range(NT), STC)
            hfm = A.alloc("a1h", [128, NC, S], BF16)
            for t in range(NT):
                h_tile(x_sb[:], t, t, hfm, t * 128, tstat,
                       extra_w={"tensor": E.cnt["tensor"]})
            qfull = A.alloc("a1qfull", [128, NC, S], BF16)
            kfm = A.alloc("a1k", [128, NC, S], BF16)
            bq_sb = A.alloc("a1bq", [128, 8], F32)
            bk_sb = A.alloc("a1bk", [128, 8], F32)
            bvd_sb = A.alloc("a1bvd", [128, 2], F32)
            w_sl = A.alloc("a1wsl", [128, 2, 8, 128], BF16)
            for d_, s_ in ((bq_sb, "a1_bq"), (bk_sb, "a1_bk"),
                           (bvd_sb, "a1_bvd")):
                E.op("sync", lambda e, d_=d_, s_=s_: e.dma_start(
                    out=d_[:], in_=ap[s_]), 16, w=A.wfirst(d_), ch="const")
            tcn = E.cnt["const"]
            tbq32 = E.op("vector", lambda e: e.tensor_scalar_mul(
                bq_sb[:], bq_sb[:], 1.0 / 32.0), 1, {"const": tcn})
            E.lab = f"{seq}:A1.q"
            tm_sl = [0, 0]
            for mc in range(8):
                sl = mc % 2
                chn = "wA" if sl == 0 else "wB"
                tw = wload(w_sl[:, sl], ap["a1_wq"][:, :, mc * 128:(mc + 1) * 128],
                           w=wmax(A.wfirst(w_sl), {"tensor": tm_sl[sl]}), ch=chn)
                for n in range(4):
                    pb = n % 2
                    wq_ = {chn: tw, "scalar": E.cnt["scalar"],
                           "vector": E.cnt["vector"]}
                    tm = mm_fm(bank(pb), w_sl[:, sl], 0, hfm, n * 512, 512,
                               range(8), wr=wq_)
                    tm_sl[sl] = tm
                    E.op("scalar", lambda e, pb=pb, mc=mc, n=n: e.activation(
                        qfull[:, mc, n * 512:(n + 1) * 512], bank(pb), AF.Identity,
                        bias=bq_sb[:, mc:mc + 1], scale=1.0 / 32.0), 1,
                        wmax(A.wfirst(qfull), {"tensor": tm, "vector": tbq32}))
            E.lab = f"{seq}:A1.k"
            for mc in range(8):
                sl = mc % 2
                chn = "wA" if sl == 0 else "wB"
                tw = wload(w_sl[:, sl], ap["a1_wk"][:, :, mc * 128:(mc + 1) * 128],
                           w=wmax(A.wfirst(w_sl), {"tensor": tm_sl[sl]}), ch=chn)
                for n in range(4):
                    pb = n % 2
                    wk_ = {chn: tw, "scalar": E.cnt["scalar"],
                           "vector": E.cnt["vector"]}
                    tm = mm_fm(bank(pb), w_sl[:, sl], 0, hfm, n * 512, 512,
                               range(8), wr=wk_)
                    tm_sl[sl] = tm
                    E.op("scalar", lambda e, pb=pb, mc=mc, n=n: e.activation(
                        kfm[:, mc, n * 512:(n + 1) * 512], bank(pb), AF.Identity,
                        bias=bk_sb[:, mc:mc + 1]), 1,
                        wmax(A.wfirst(kfm), {"tensor": tm, "const": tcn}))
            E.lab = f"{seq}:A1.v"
            vd = A.alloc("a1vd", [128, 2, S], BF16)
            wvd_sb = A.alloc("a1wvd", [128, 8, 256], BF16)
            twvd = wload(wvd_sb[:], ap["a1_wvd"], w=A.wfirst(wvd_sb), ch="w0")
            for mc in range(2):
                for n in range(4):
                    pb = 2 + n % 2
                    wv_ = {"w0": twvd, "scalar": E.cnt["scalar"],
                           "vector": E.cnt["vector"]}
                    tm = mm_fm(bank(pb), wvd_sb, mc, hfm, n * 512, 512,
                               range(8), wr=wv_)
                    E.op("scalar", lambda e, pb=pb, mc=mc, n=n: e.activation(
                        vd[:, mc, n * 512:(n + 1) * 512], bank(pb), AF.Identity,
                        bias=bvd_sb[:, mc:mc + 1]), 1,
                        wmax(A.wfirst(vd), {"tensor": tm}))
            A.free(hfm)
            A.free(wvd_sb)
            vtok = A.alloc("a1v", [128, NT, DM], BF16)
            vu_sb = A.alloc("a1vu", [128, 2, 1024], BF16)
            bvu_b = A.alloc("a1bvu", [128, 1024], F32)
            tvu = wload(vu_sb[:], ap["a1_vu"], w=A.wfirst(vu_sb), ch="w0")
            tbvu = E.op("sync", lambda e: e.dma_start(
                out=bvu_b[:], in_=ap["a1_bvu"]), 16, w=A.wfirst(bvu_b), ch="const")
            for t in range(NT):
                pb = 2 * (t % 2)
                wv2 = {"w0": tvu, "vector": E.cnt["vector"],
                       "scalar": E.cnt["scalar"]}
                def vmm(e, t=t, pb=pb):
                    last = None
                    for rc in range(2):
                        for half in range(2):
                            last = e.matmul(
                                PS[:, 512 * pb + half * 512: 512 * pb + half * 512 + 512],
                                vd[:, rc, t * 128:(t + 1) * 128],
                                vu_sb[:, rc, half * 512:half * 512 + 512],
                                start=(rc == 0), stop=(rc == 1))
                    return last
                tv = E.op("tensor", vmm, 1, wv2)
                E.op("vector", lambda e, t=t, pb=pb: e.scalar_tensor_tensor(
                    vtok[:, t, :], PS[:, 512 * pb:512 * pb + 1024], 1.0,
                    bvu_b[:], ALU.mult, ALU.add), 1,
                    wmax(A.wfirst(vtok), {"tensor": tv, "const": tbvu}))
            A.free(vd)
            A.free(vu_sb)
            A.free(bvu_b)
            A.free(bvd_sb)
            A.free(w_sl)
            A.free(bq_sb)
            A.free(bk_sb)
            att2 = A.alloc("a1att", [128, 4, S], BF16)
            attTg = A.alloc("a1attT", [128, NT, 256], BF16)
            ystage = A.alloc("a1y", [128, NC, 256], F32)
            dg3 = A.alloc("a1dg", [128, 4, 128], BF16)
            attn_sync_banks()
            for g in range(NT // 2):
                E.lab = f"{seq}:A1.att"
                qgv = qfull[:, :, g * 256:(g + 1) * 256]

                def a1_evac(hc, aoff, wav):
                    E.op("vector", lambda e, hc=hc, aoff=aoff: e.tensor_copy(
                        ystage[:, hc, :], bank(7)[:, aoff:aoff + 256]), 1,
                        wmax(A.wfirst(ystage), wav), nm="oe")
                attn_core(A_HEADS, 2, 2 * g, qgv, kfm, vtok, False, att2, attTg,
                          dg3, a1_evac, depth=4)
                E.lab = f"{seq}:A1.post"
                postln_group(ystage, 2 * g, n_tiles=2)
                attn_sync_banks()
            for b_ in (kfm, vtok, qfull, att2, attTg, dg3, ystage, idf):
                A.free(b_)

        def emit_F1(seq):
            E.lab = f"{seq}:F1.h"
            idf = A.alloc("idf", [128, 128], F32)
            E.op("sync", lambda e: e.dma_start(out=idf[:], in_=ap["idf"]), 16,
                 w=A.wfirst(idf), ch="const")
            postln_idf[0] = idf
            tstat = ln_stats(x_sb[:], range(NT), STC)
            hfm = A.alloc("f1h", [128, NC, S], BF16)
            for t in range(NT):
                h_tile(x_sb[:], t, t, hfm, t * 128, tstat,
                       extra_w={"tensor": E.cnt["tensor"]})
            basew = A.alloc("f1bw", [128, 8, 1024], BF16)
            s96 = A.alloc("f1s96", [128, 8, 96], BF16)
            sa16 = A.alloc("f1sa", [128, 8, 16], BF16)
            sbt = A.alloc("f1sbt", [128, 1024], BF16)
            btb = A.alloc("f1btb", [128, 1024], BF16)
            rt = A.alloc("f1rt", [128, 8, 6], BF16)
            sel = A.alloc("f1sel", [128, 96], BF16)
            rb = A.alloc("f1rb", [128, 6], F32)
            bb = A.alloc("f1bb", [128, 1024], F32)
            iota = A.alloc("f1io", [128, 96], F32)
            big = A.alloc("f1big", [128, 96], F32)

            her = A.alloc("f1her", [128, S], BF16)
            sht = A.alloc("f1sht", [128, S], BF16)
            hsc = A.alloc("f1hsc", [128, S], BF16)
            gl = A.alloc("f1gl", [128, 6, 96], F32)
            wfm = A.alloc("f1wfm", [128, NT, 128], BF16)
            ystg = A.alloc("f1ystg", [128, NT, DM], BF16)
            f1bank = [{}, {}]
            tbw = wload(basew[:], ap["f1_basewt"], w=A.wfirst(basew), ch="w0")
            t96 = wload(s96[:], ap["f1_s96"], w=A.wfirst(s96), ch="w0")
            tsa = wload(sa16[:], ap["f1_sa16"], w=A.wfirst(sa16), ch="w0")
            tsb = wload(sbt[:], ap["f1_sbt"], w=A.wfirst(sbt), ch="w0")
            tbt = wload(btb[:], ap["f1_btb"], w=A.wfirst(btb), ch="w0")
            trt = wload(rt[:], ap["f1_rt"], w=A.wfirst(rt), ch="w0")
            tsl = wload(sel[0:32, :], ap["f1_sel"], w=A.wfirst(sel), ch="w0")
            for d_, s_ in ((rb, "f1_rb"), (bb, "f1_baseb"), (iota, "iota6"),
                           (big, "big6")):
                E.op("sync", lambda e, d_=d_, s_=s_: e.dma_start(
                    out=d_[:], in_=ap[s_]), 16, w=A.wfirst(d_), ch="const")
            tcn = E.cnt["const"]

            E.lab = f"{seq}:F1.down"
            for n in range(4):
                pb = n % 2
                we_ = {"w0": E.cnt["w0"], "scalar": E.cnt["scalar"],
                       "vector": E.cnt["vector"]}
                def emm(e, n=n, pb=pb):
                    last = None
                    for kc in range(8):
                        last = e.matmul(
                            bank(pb)[0:96, :], s96[:, kc, :],
                            hfm[:, kc, n * 512:(n + 1) * 512],
                            start=(kc == 0), stop=(kc == 7))
                    return last
                te = E.op("tensor", emm, 1, we_)
                E.op("vector", lambda e, n=n, pb=pb: e.tensor_copy(
                    her[0:96, n * 512:(n + 1) * 512], bank(pb)[0:96, :]), 1,
                    wmax(A.wfirst(her), {"tensor": te}))
                def smm(e, n=n, pb=pb):
                    last = None
                    for kc in range(8):
                        last = e.matmul(
                            bank(2 + pb)[0:16, :], sa16[:, kc, :],
                            hfm[:, kc, n * 512:(n + 1) * 512],
                            start=(kc == 0), stop=(kc == 7))
                    return last
                ts = E.op("tensor", smm, 1)
                E.op("vector", lambda e, n=n, pb=pb: e.tensor_copy(
                    sht[0:16, n * 512:(n + 1) * 512], bank(2 + pb)[0:16, :]), 1,
                    wmax(A.wfirst(sht), {"tensor": ts}))
            E.lab = f"{seq}:F1.rt"
            glog = gl[:, 0, :].rearrange("p (t k) -> p t k", k=6)
            gate = gl[:, 1, :].rearrange("p (t k) -> p t k", k=6)
            scr1 = gl[:, 2, :].rearrange("p (t k) -> p t k", k=6)
            scr2 = gl[:, 3, :].rearrange("p (t k) -> p t k", k=6)
            keep = gl[:, 4, :].rearrange("p (t k) -> p t k", k=6)
            wtok = gl[:, 5, :].rearrange("p (t k) -> p t k", k=6)
            for t in range(NT):
                wr_ = {"w0": trt, "vector": E.cnt["vector"]}
                def rmm(e, t=t):
                    last = None
                    for kc in range(8):
                        last = e.matmul(
                            bank(7)[:, 0:6], hfm[:, kc, t * 128:(t + 1) * 128],
                            rt[:, kc, :], start=(kc == 0), stop=(kc == 7))
                    return last
                tr = E.op("tensor", rmm, 1, wr_)
                E.op("vector", lambda e, t=t: e.scalar_tensor_tensor(
                    glog[:, t, :], bank(7)[:, 0:6], 1.0, rb[:],
                    ALU.mult, ALU.add), 1,
                    wmax(A.wfirst(gl), {"tensor": tr, "const": tcn}))
            tgl = E.cnt["vector"]
            E.lab = f"{seq}:F1.gates"
            c1 = 64
            g1 = E.op("vector", lambda e: e.tensor_reduce(
                st[:, c1:c1 + 16], glog, axis=AX.X, op=ALU.max, negate=True),
                1, sw=tgl)
            g2 = E.op("vector", lambda e: e.tensor_tensor(
                scr1, glog,
                st[:, c1:c1 + 16].rearrange("p (t o) -> p t o", o=1).broadcast_to([128, NT, 6]),
                ALU.add), 1, sw=g1)
            g3 = E.op("scalar", lambda e: e.activation(
                gl[:, 1, :], gl[:, 2, :], AF.Exp), 1, {"vector": g2})
            g4 = E.op("vector", lambda e: e.tensor_reduce(
                st[:, c1:c1 + 16], gate, axis=AX.X, op=ALU.add), 1, {"scalar": g3})
            g5 = E.op("vector", lambda e: e.reciprocal(
                st[:, c1:c1 + 16], st[:, c1:c1 + 16]), 1, sw=g4)
            g6 = E.op("vector", lambda e: e.tensor_tensor(
                gate, gate,
                st[:, c1:c1 + 16].rearrange("p (t o) -> p t o", o=1).broadcast_to([128, NT, 6]),
                ALU.mult), 1, sw=g5)
            g7 = E.op("vector", lambda e: e.tensor_reduce(
                st[:, c1 + 16:c1 + 32], gate, axis=AX.X, op=ALU.max), 1, sw=g6)
            g8 = E.op("vector", lambda e: e.tensor_tensor(
                scr1, gate,
                st[:, c1 + 16:c1 + 32].rearrange("p (t o) -> p t o", o=1).broadcast_to([128, NT, 6]),
                ALU.is_ge), 1, sw=g7)
            g9a = E.op("vector", lambda e: e.tensor_tensor(
                gl[:, 3, :], gl[:, 2, :], big[:], ALU.mult), 1, sw=g8)
            g9 = E.op("vector", lambda e: e.tensor_scalar_add(
                gl[:, 3, :], gl[:, 3, :], 100.0), 1, sw=g9a)
            g10 = E.op("vector", lambda e: e.tensor_reduce(
                st[:, c1 + 32:c1 + 48], scr2, axis=AX.X, op=ALU.min), 1, sw=g9)
            g11 = E.op("vector", lambda e: e.tensor_tensor(
                scr1, iota[:].rearrange("p (t k) -> p t k", k=6),
                st[:, c1 + 32:c1 + 48].rearrange("p (t o) -> p t o", o=1).broadcast_to([128, NT, 6]),
                ALU.not_equal), 1, sw=g10)
            g12 = E.op("vector", lambda e: e.tensor_tensor(
                gl[:, 3, :], gl[:, 2, :], gl[:, 1, :], ALU.mult), 1, sw=g11)
            g13 = E.op("vector", lambda e: e.tensor_reduce(
                st[:, c1 + 48:c1 + 64], scr2, axis=AX.X, op=ALU.max), 1, sw=g12)
            g14 = E.op("vector", lambda e: e.tensor_tensor(
                keep, scr2,
                st[:, c1 + 48:c1 + 64].rearrange("p (t o) -> p t o", o=1).broadcast_to([128, NT, 6]),
                ALU.is_ge), 1, sw=g13)
            g15 = E.op("vector", lambda e: e.tensor_tensor(
                scr1, iota[:].rearrange("p (t k) -> p t k", k=6),
                st[:, c1 + 32:c1 + 48].rearrange("p (t o) -> p t o", o=1).broadcast_to([128, NT, 6]),
                ALU.is_equal), 1, sw=g14)
            g16 = E.op("vector", lambda e: e.tensor_max(
                keep, keep, scr1), 1, sw=g15)
            g17 = E.op("vector", lambda e: e.tensor_add(
                st[:, c1 + 16:c1 + 32], st[:, c1 + 16:c1 + 32],
                st[:, c1 + 48:c1 + 64]), 1, sw=g16)
            g18 = E.op("vector", lambda e: e.tensor_scalar_add(
                st[:, c1 + 16:c1 + 32], st[:, c1 + 16:c1 + 32], 1e-6), 1, sw=g17)
            g19 = E.op("vector", lambda e: e.reciprocal(
                st[:, c1 + 16:c1 + 32], st[:, c1 + 16:c1 + 32]), 1, sw=g18)
            g20 = E.op("vector", lambda e: e.tensor_tensor(
                wtok, gate, keep, ALU.mult), 1, sw=g19)
            g21 = E.op("vector", lambda e: e.tensor_tensor(
                wtok, wtok,
                st[:, c1 + 16:c1 + 32].rearrange("p (t o) -> p t o", o=1).broadcast_to([128, NT, 6]),
                ALU.mult), 1, sw=g20)
            for t in range(NT):
                E.lab = f"{seq}:F1.tile"
                wtr = {"vector": g21, "scalar": E.cnt["scalar"]}
                ttw = E.op("tensor", lambda e, t=t: e.transpose(
                    bank(7)[0:6, 0:128], gl[:, 5, t * 6:(t + 1) * 6],
                    postln_idf[0][:]), 1, wtr)
                tev = E.op("vector", lambda e, t=t: e.tensor_copy(
                    wfm[0:6, t, :], bank(7)[0:6, 0:128]), 1,
                    wmax(A.wfirst(wfm), {"tensor": ttw}))
                t96_ = E.op("tensor", lambda e, t=t: e.matmul(
                    bank(6)[0:96, 0:128], sel[0:6, :], wfm[0:6, t, :],
                    start=True, stop=True), 1,
                    {"vector": tev, "w0": tsl})
                ths = E.op("vector", lambda e, t=t: e.tensor_tensor(
                    hsc[0:96, t * 128:(t + 1) * 128],
                    her[0:96, t * 128:(t + 1) * 128],
                    bank(6)[0:96, 0:128], ALU.mult), 1,
                    wmax(A.wfirst(hsc), {"tensor": t96_}))
                pbt = 2 + 2 * (t % 2)
                ps_tok = PS[:, 512 * pbt:512 * pbt + 1024]
                wup = wmax({"vector": ths, "w0": tbt}, f1bank[t % 2])
                def upmm(e, t=t, ps0=512 * pbt):
                    last = None
                    for half in range(2):
                        dst = PS[:, ps0 + half * 512: ps0 + half * 512 + 512]
                        for kc in range(8):
                            e.matmul(dst, hfm[:, kc, t * 128:(t + 1) * 128],
                                     basew[:, kc, half * 512:half * 512 + 512],
                                     start=(kc == 0), stop=False)
                        e.matmul(dst, sht[0:16, t * 128:(t + 1) * 128],
                                 sbt[0:16, half * 512:half * 512 + 512],
                                 start=False, stop=False)
                        last = e.matmul(
                            dst, hsc[0:96, t * 128:(t + 1) * 128],
                            btb[0:96, half * 512:half * 512 + 512],
                            start=False, stop=True)
                    return last
                tup = E.op("tensor", upmm, 1, wup)
                tbb = E.op("vector", lambda e, ps_tok=ps_tok: e.tensor_tensor(
                    ps_tok, ps_tok, bb[:], ALU.add), 1,
                    {"tensor": tup, "const": tcn})
                ceng = "scalar" if t % 2 == 0 else "vector"
                tcp = E.op(ceng, lambda e, t=t, ps_tok=ps_tok, ceng=ceng: (
                    e.copy if ceng == "scalar" else e.tensor_copy)(
                        ystg[:, t, :], ps_tok), 1,
                    wmax(A.wfirst(ystg), {"vector": tbb}))
                f1bank[t % 2] = {ceng: tcp}
            postln_tok(ystg)
            for b_ in (hfm, basew, s96, sa16, sbt, btb, rt, sel, rb, bb, iota,
                       big, her, sht, hsc, gl, wfm, idf, ystg):
                A.free(b_)

        def postln_psum_pair(ps_tok, t, wsrc):
            c0 = 64
            tsum = E.op("vector", lambda e: e.reduce_sum(
                st[:, c0:c0 + 1], ps_tok, axis=AX.X), 1, wsrc,
                sw=E.cnt["vector"])
            tsq = E.op("scalar", lambda e: e.activation(
                sq_scr[:], ps_tok, AF.Square, accum_out=st[:, c0 + 1:c0 + 2]), 1,
                wmax(wsrc, {"vector": E.cnt["vector"]}), sw=E.cnt["scalar"])
            t1 = E.op("vector", lambda e: e.tensor_scalar_mul(
                st[:, c0:c0 + 1], st[:, c0:c0 + 1], 1.0 / DM), 1,
                {"scalar": tsq}, sw=tsum)
            t2 = E.op("vector", lambda e: e.tensor_scalar_mul(
                st[:, c0 + 1:c0 + 2], st[:, c0 + 1:c0 + 2], 1.0 / DM), 1, sw=t1)
            t3 = E.op("vector", lambda e: e.scalar_tensor_tensor(
                st[:, 240:241], st[:, c0:c0 + 1], 1.0,
                st[:, c0:c0 + 1], ALU.mult, ALU.mult), 1, sw=t2)
            t4 = E.op("vector", lambda e: e.tensor_sub(
                st[:, c0 + 1:c0 + 2], st[:, c0 + 1:c0 + 2], st[:, 240:241]), 1, sw=t3)
            tl = E.op("scalar", lambda e: e.activation(
                st[:, c0 + 2:c0 + 3], st[:, c0 + 1:c0 + 2], AF.Ln,
                bias=cst[:, 0:1]), 1, {"vector": t4})
            ts2 = E.op("scalar", lambda e: e.activation(
                st[:, c0 + 2:c0 + 3], st[:, c0 + 2:c0 + 3], AF.Exp, scale=-0.5),
                1, sw=tl)
            t5 = E.op("vector", lambda e: e.scalar_tensor_tensor(
                st[:, c0 + 3:c0 + 4], st[:, c0:c0 + 1], -1.0,
                st[:, c0 + 2:c0 + 3], ALU.mult, ALU.mult), 1, {"scalar": ts2}, sw=t4)
            ta = E.op("scalar", lambda e: e.activation(
                sq_scr[:], ps_tok, AF.Identity,
                bias=st[:, c0 + 3:c0 + 4], scale=st[:, c0 + 2:c0 + 3]), 1,
                {"vector": t5}, sw=ts2)
            return E.op("vector", lambda e, t=t: e.tensor_add(
                x_sb[:, t, :], x_sb[:, t, :], sq_scr[:]), 1, {"scalar": ta})

        BLK = ent(nc.Block())
        for seq in range(SPC):
            E.lab = f"{seq}:xio"
            wld = E.tok()
            tx = E.op("sync", lambda e, seq=seq: e.dma_start(
                out=x_sb[:],
                in_=ap["x_in"][seq].rearrange("(t p) d -> p t d", p=128)), 16, wld, ch="xio")
            for b in blocks:
                if b == "LN0":
                    tstat = ln_stats(x_sb[:], range(NT), STC)
                    dump("dbg_stats", st[:, 0:64], tstat)
                elif b == "A0":
                    emit_A0(seq)
                elif b == "F0":
                    emit_F0(seq)
                elif b == "A1":
                    emit_A1(seq)
                elif b == "F1":
                    emit_F1(seq)
            wst = E.tok()
            E.op("sync", lambda e, seq=seq: e.dma_start(
                out=x_out[seq].rearrange("(t p) d -> p t d", p=128),
                in_=x_sb[:]), 16, wst, ch="xio")
        E.emit(BLK, sems)
    global LAST_EM
    LAST_EM = E
    counts = {e: len(E.q[e]) for e in ENG}
    print(f"build_nc({blocks}): ops={counts} sbuf_peak={A.peak}")
    return nc



def kernel(**inputs):
    shared = prep_inputs(inputs)
    x = np.asarray(inputs["L0_attn_x"])
    core_maps = []
    for c in range(NCORES):
        m = dict(shared)
        m["x_in"] = np.ascontiguousarray(x[c * SPC:(c + 1) * SPC])
        core_maps.append(m)

    if FUSED:
        nc = build_nc(["A0", "F0", "A1", "F1"])
        res = run_bass_kernel_spmd(nc, core_maps, core_ids=list(range(NCORES)))
        outs = [r["x_out"] for r in res.results]
    else:
        outs = None
        for blocks in (["A0"], ["F0"], ["A1"], ["F1"]):
            nc = build_nc(blocks)
            res = run_bass_kernel_spmd(nc, core_maps, core_ids=list(range(NCORES)))
            outs = [np.asarray(r["x_out"]) for r in res.results]
            for c in range(NCORES):
                core_maps[c]["x_in"] = np.ascontiguousarray(outs[c])
        assert outs is not None
    return np.concatenate([np.asarray(o, np.float32) for o in outs], 0)
